# version 17
# baseline (speedup 1.0000x reference)
import contextlib
import numpy as np
import concourse.bass as bass
import concourse.mybir as mybir
from concourse.bass_utils import run_bass_kernel_spmd

F32 = mybir.dt.float32
BF16 = mybir.dt.bfloat16
I32 = mybir.dt.int32
AF = mybir.ActivationFunctionType
ALU = mybir.AluOpType

D = 4096
T = 1024
NT = 8
ND = 32
NH = 32
NKH = 8
HID = 11008
NHC = 86
KV = 4096
NKC = 32
INVSQ = 1.0 / np.sqrt(128.0)
EPS = 1e-5

ENGS = ("sync", "gpsimd", "tensor", "scalar", "vector")

OGS = [(i, min(3, 32 - i)) for i in range(0, 32, 3)]
W2GRP = [(0, 22), (22, 22), (44, 22), (66, 20)]
GTGRP = [(i, min(8, NHC - i)) for i in range(0, NHC, 8)]


class Bld:
    def __init__(self, nc, stack):
        self.nc = nc
        self.ops = {e: [] for e in ENGS}
        self.sems = {}
        self.stack = stack
        self.dmaseq = {}

    def _sem(self, name):
        if name not in self.sems:
            h = self.stack.enter_context(self.nc.semaphore(name))
            self.sems[name] = [h, 0, 0]
        return self.sems[name]

    def op(self, eng, fn, inc=None, k=1):
        if inc is None:
            self.ops[eng].append(fn)
            return None
        rec = self._sem(inc)
        rec[1] += k
        tgt = rec[1]
        h = rec[0]

        def f(E, fn=fn, h=h, k=k):
            fn(E).then_inc(h, k)
        self.ops[eng].append(f)
        return (inc, tgt)

    def dma(self, eng, fn, inc, parity=True, nway=2):
        if parity:
            seq = self.dmaseq.get(inc, 0)
            self.dmaseq[inc] = seq + 1
            inc = f"{inc}{seq % nway}"
        return self.op(eng, fn, inc, 16)

    def wait(self, eng, tok):
        if tok is None:
            return
        name, v = tok
        rec = self._sem(name)
        rec[2] = max(rec[2], v)
        h = rec[0]
        self.ops[eng].append(lambda E, h=h, v=v: E.wait_ge(h, v))

    def drain(self, eng):
        self.ops[eng].append(lambda E: E.drain())

    def barrier(self):
        rec = self._sem("ph")
        tgt = rec[1] + len(ENGS)
        h = rec[0]
        for e in ENGS:
            self.ops[e].append(lambda E: E.drain())
            self.ops[e].append(lambda E, h=h: E.sem_inc(h, 1))
            self.ops[e].append(lambda E, h=h, v=tgt: E.wait_ge(h, v))
        rec[1] = tgt
        rec[2] = max(rec[2], tgt)

    def check(self):
        for name, (h, cnt, maxw) in self.sems.items():
            assert maxw <= cnt, f"sem {name}: waited {maxw} > produced {cnt}"


def build_program():
    nc = bass.Bass("TRN2", target_bir_lowering=False)

    xd = nc.dram_tensor("x", [T, D], F32, kind="ExternalInput").ap()
    yd = nc.dram_tensor("y", [T, D], F32, kind="ExternalOutput").ap()
    wqP = nc.dram_tensor("wqP", [32, 128, ND * 128], F32, kind="ExternalInput").ap()
    woP = nc.dram_tensor("woP", [32, 128, ND * 128], F32, kind="ExternalInput").ap()
    w1P = nc.dram_tensor("w1P", [NHC, 128, ND * 128], F32, kind="ExternalInput").ap()
    w3P = nc.dram_tensor("w3P", [NHC, 128, ND * 128], F32, kind="ExternalInput").ap()
    w2P = nc.dram_tensor("w2P", [128, NHC, D], F32, kind="ExternalInput").ap()
    a1d = nc.dram_tensor("a1P", [128, ND, 16], F32, kind="ExternalInput").ap()
    a3d = nc.dram_tensor("a3P", [128, ND, 16], F32, kind="ExternalInput").ap()
    a2d = nc.dram_tensor("a2P", [128, NHC, 16], F32, kind="ExternalInput").ap()
    b1d = nc.dram_tensor("b1T", [16, HID], F32, kind="ExternalInput").ap()
    b3d = nc.dram_tensor("b3T", [16, HID], F32, kind="ExternalInput").ap()
    b2d = nc.dram_tensor("b2T", [16, D], F32, kind="ExternalInput").ap()
    kcd = nc.dram_tensor("kc", [512, 16384], F32, kind="ExternalInput").ap()
    vcd = nc.dram_tensor("vc", [512, 16384], F32, kind="ExternalInput").ap()
    btd = nc.dram_tensor("bt", [2, 128], I32, kind="ExternalInput").ap()
    cosd = nc.dram_tensor("cosT", [128, T], F32, kind="ExternalInput").ap()
    sind = nc.dram_tensor("sinTs", [128, T], F32, kind="ExternalInput").ap()
    idnd = nc.dram_tensor("idn", [128, 128], F32, kind="ExternalInput").ap()
    constd = nc.dram_tensor("consts", [128, 8], F32, kind="ExternalInput").ap()
    onesrd = nc.dram_tensor("onesr", [1, 128], F32, kind="ExternalInput").ap()

    gts = nc.dram_tensor("gts", [NHC, 128, T], BF16).ap()
    x1s = nc.dram_tensor("x1s", [NT, 128, D], F32).ap()

    A_OFF = 32736
    B_OFF = A_OFF + 65536
    C_OFF = B_OFF + 65536
    assert C_OFF + 65536 <= 229344

    al = nc.alloc_sbuf_tensor_at
    lnT = al("lnT", [128, ND, T], BF16, offset=A_OFF)
    Kraw = al("Kraw", [128, 2, 2, 2048], BF16, offset=A_OFF)
    Vraw = al("Vraw", [128, 2, 2, 2048], BF16, offset=A_OFF + 16384)
    KgT = al("KgT", [128, 2, KV], BF16, offset=A_OFF + 32768)
    expb = al("expb", [128, 4, T], BF16, offset=A_OFF + 49152)
    padd = al("padd", [128, 2, T], BF16, offset=A_OFF + 57344)
    rb_sb = al("rb_sb", [128, T], BF16, offset=A_OFF + 61440)
    r_sb = al("r_sb", [1, T], BF16, offset=A_OFF + 63488)
    wslabB = al("wslabB", [128, 2, ND, 128], BF16, offset=A_OFF)
    aoT = al("aoT", [128, 2, T], BF16, offset=A_OFF + 16384)
    gt_rd = al("gt_rd", [128, 2, 8, T], BF16, offset=A_OFF)
    b2sb = al("b2sb", [16, D], BF16, offset=A_OFF + 32768)
    qctx = al("qctx", [128, NH, T], BF16, offset=B_OFF)
    xst = al("xst", [128, 3, D], F32, offset=B_OFF)
    ln1t = al("ln1t", [128, 2, D], BF16, offset=B_OFF + 49152)
    lnA = al("lnA", [128, D], F32, offset=B_OFF + 49152)
    x1t = al("x1t", [128, D], F32, offset=B_OFF + 32768)
    s1 = al("s1", [128, T], F32, offset=B_OFF)
    gtt = al("gtt", [128, 2, T], BF16, offset=B_OFF + 4096)
    w2slab = al("w2slab", [128, 2, 22, 384], BF16, offset=B_OFF)
    oTsb = al("oTsb", [128, 2, T], BF16, offset=B_OFF + 33792)
    x1rd = al("x1rd", [128, 2, D], F32, offset=B_OFF)
    lnF = al("lnF", [128, D], F32, offset=B_OFF + 32768)
    outT = al("outT", [128, D], F32, offset=B_OFF + 49152)
    raw = al("raw", [128, NT, D], BF16, offset=C_OFF)
    cosT = al("cosT", [128, T], F32, offset=C_OFF)
    sinTs = al("sinTs", [128, T], F32, offset=C_OFF + 4096)
    wslabA = al("wslabA", [128, 2, ND, 128], BF16, offset=C_OFF + 8192)
    t1sb = al("t1sb", [128, T], F32, offset=C_OFF + 24576)
    t2sb = al("t2sb", [128, T], F32, offset=C_OFF + 28672)
    w1slab = al("w1slab", [128, 2, ND, 128], BF16, offset=C_OFF)
    w3slab = al("w3slab", [128, 2, ND, 128], BF16, offset=C_OFF + 16384)
    a1sb = al("a1sb", [128, ND, 16], BF16, offset=C_OFF + 32768)
    a3sb = al("a3sb", [128, ND, 16], BF16, offset=C_OFF + 33792)
    a2sb = al("a2sb", [128, NHC, 16], BF16, offset=C_OFF + 34816)

    consts = nc.alloc_sbuf_tensor("consts_sb", [128, 8], F32)
    idn = nc.alloc_sbuf_tensor("idn_sb", [128, 128], BF16)
    bt0 = nc.alloc_sbuf_tensor("bt0", [128, 1], I32)
    bt1 = nc.alloc_sbuf_tensor("bt1", [128, 1], I32)
    onesr = nc.alloc_sbuf_tensor("onesr_sb", [1, 128], F32)
    onesrb = nc.alloc_sbuf_tensor("onesrb", [1, 128], BF16)
    onesc = nc.alloc_sbuf_tensor("onesc", [128, 1], BF16)
    STS = []
    for i in range(10):
        STS.append(dict(
            st=nc.alloc_sbuf_tensor(f"st{i}", [128, 8, 6], F32),
            mv=nc.alloc_sbuf_tensor(f"mv{i}", [128, 2], F32),
            rstd=nc.alloc_sbuf_tensor(f"rstd{i}", [128, 1], F32),
            rtmp=nc.alloc_sbuf_tensor(f"rtmp{i}", [128, 1], F32),
            mr=nc.alloc_sbuf_tensor(f"mr{i}", [128, 1], F32),
        ))
    xg2 = nc.alloc_sbuf_tensor("xg2", [16, T], BF16)
    sacc = nc.alloc_sbuf_tensor("sacc", [128, T], F32)
    xa1 = al("xa1", [16, T], BF16, offset=C_OFF + 37888)
    xa3 = al("xa3", [16, T], BF16, offset=C_OFF + 39936)
    b1st = al("b1st", [16, 2, 512], BF16, offset=C_OFF + 41984)
    b3st = al("b3st", [16, 2, 512], BF16, offset=C_OFF + 44032)
    assert nc.sbuf_base <= A_OFF, f"bump overflow: {nc.sbuf_base} > {A_OFF}"

    one_ap = consts.ap()[:, 0:1]
    eps_ap = consts.ap()[:, 1:2]
    invsq_ap = consts.ap()[:, 2:3]
    neghalf_ap = consts.ap()[:, 3:4]
    two_ap = consts.ap()[:, 4:5]
    zero_ap = consts.ap()[:, 5:6]

    PP = [nc.alloc_psum_tensor(f"PP{i}", [128, 1024], F32) for i in range(4)]
    PA = [pp.ap() for pp in PP]
    P = []
    for i in range(4):
        P.append(PA[i][:, 0:512])
        P.append(PA[i][:, 512:1024])
    PBF = [PA[i].bitcast(BF16) for i in range(4)]

    stack = contextlib.ExitStack()
    b = Bld(nc, stack)

    xr = xd.rearrange("(tc p) d -> tc p d", p=128)
    yr = yd.rearrange("(tc p) d -> tc p d", p=128)

    for dst, src in ((consts.ap(), constd), (idn.ap(), idnd),
                     (bt0.ap(), btd[0].rearrange("(p one) -> p one", one=1)),
                     (bt1.ap(), btd[1].rearrange("(p one) -> p one", one=1)),
                     (onesr.ap(), onesrd), (onesc.ap(), onesrd.rearrange("one p -> p one")),
                     (onesrb.ap(), onesrd),
                     (cosT.ap(), cosd), (sinTs.ap(), sind)):
        b.dma("gpsimd", lambda E, dst=dst, src=src: E.dma_start(dst, src), "ld", parity=False)
    b.wait("gpsimd", ("ld", 9 * 16))
    b.barrier()

    def ln_stats(src_ap_fn, sem_pfx, si):
        S = STS[si]
        b.drain("vector")
        for g in range(8):
            b.op("vector", lambda E, g=g, f=src_ap_fn, S=S: E.bn_stats(S["st"].ap()[:, g, :], f(g)))
        b.drain("vector")
        tva = b.op("vector", lambda E, S=S: E.bn_aggr(S["mv"].ap(), S["st"].ap()[:, :, :]),
                   sem_pfx + "va")
        b.wait("scalar", tva)
        b.drain("scalar")
        b.op("scalar", lambda E, S=S: E.activation(S["rtmp"].ap(), S["mv"].ap()[:, 1:2], AF.Ln,
                                                   bias=eps_ap, scale=one_ap))
        b.drain("scalar")
        b.op("scalar", lambda E, S=S: E.activation(S["rstd"].ap(), S["rtmp"].ap(), AF.Exp,
                                                   bias=zero_ap, scale=neghalf_ap))
        b.drain("scalar")
        tsc = b.op("scalar", lambda E, S=S: E.activation(S["mr"].ap(), S["mv"].ap()[:, 0:1], AF.Copy,
                                                         bias=0.0, scale=S["rstd"].ap()), sem_pfx + "sr")
        return tsc

    def transpose_to_lnT(src_ap_fn, tc, tok_ready, tok_grp, tok_ev, pfx):
        for g in range(2):
            b.wait("tensor", tok_ready)
            if (tc, g) != (0, 0):
                prev = (tc, g - 1) if g > 0 else (tc - 1, 1)
                b.wait("tensor", tok_ev[prev])
            for j in range(16):
                dc = g * 16 + j
                tok = b.op("tensor", lambda E, dc=dc, j=j, f=src_ap_fn: E.transpose(
                    PBF[3][:, j * 128:(j + 1) * 128], f(dc), idn.ap()),
                    pfx + "t" if j == 15 else None)
            tok_grp[(tc, g)] = tok
            b.wait("vector", tok_grp[(tc, g)])
            tok_ev[(tc, g)] = b.op("vector", lambda E, tc=tc, g=g: E.tensor_copy(
                lnT.ap()[:, g * 16:(g + 1) * 16, tc * 128:(tc + 1) * 128],
                PBF[3].rearrange("p (a c) -> p a c", a=16)), pfx + "e")

    tok_x = {}
    tok_lnapply = {}
    tok_trgrp = {}
    tok_trev = {}
    tok_p1sc = {}
    for it in range(NT + 2):
        if it < NT:
            tc = it
            if tc >= 3:
                b.wait("sync", tok_lnapply.get(tc - 3))
            tok_x[tc] = b.dma("sync", lambda E, tc=tc: E.dma_start(xst.ap()[:, tc % 3, :], xr[tc]),
                              "xin", nway=3)
            b.wait("vector", tok_x[tc])
            tok_p1sc[tc] = ln_stats(lambda g, tc=tc: xst.ap()[:, tc % 3, g * 512:(g + 1) * 512],
                                    "p1", tc % 3)
        if it >= 2:
            tc = it - 2
            S = STS[tc % 3]
            b.wait("vector", tok_p1sc[tc])
            if tc >= 2:
                b.wait("vector", tok_trgrp[(tc - 2, 1)])
            tok_lnapply[tc] = b.op("vector", lambda E, tc=tc, S=S: E.tensor_scalar(
                ln1t.ap()[:, tc % 2, :], xst.ap()[:, tc % 3, :], S["mv"].ap()[:, 0:1],
                S["rstd"].ap(), op0=ALU.subtract, op1=ALU.mult), "p1ap")
            transpose_to_lnT(lambda dc, tc=tc: ln1t.ap()[:, tc % 2, dc * 128:(dc + 1) * 128],
                             tc, tok_lnapply[tc], tok_trgrp, tok_trev, "p1")
    tok_wq_pre = {}
    for oc in range(2):
        tok_wq_pre[oc] = b.dma("gpsimd", lambda E, oc=oc: E.dma_start(
            wslabA.ap()[:, oc % 2, :, :], wqP[oc].rearrange("p (dc o) -> p dc o", dc=ND)), "wq")
    b.barrier()

    tok_teq = {}
    tok_veq = {}
    for oc in range(NH):
        if oc < 2:
            tok_slab = tok_wq_pre[oc]
        else:
            b.wait("gpsimd", tok_teq[oc - 2])
            tok_slab = b.dma("gpsimd", lambda E, oc=oc: E.dma_start(
                wslabA.ap()[:, oc % 2, :, :], wqP[oc].rearrange("p (dc o) -> p dc o", dc=ND)), "wq")
        b.wait("tensor", tok_slab)
        if oc >= 2:
            b.wait("tensor", tok_veq[oc - 2])
        pi = oc % 2
        for dc in range(ND):
            b.op("tensor", lambda E, oc=oc, dc=dc, pi=pi: E.matmul(
                P[2 * pi], wslabA.ap()[:, oc % 2, dc, :], lnT.ap()[:, dc, 0:512],
                start=(dc == 0), stop=(dc == ND - 1)))
            tok = b.op("tensor", lambda E, oc=oc, dc=dc, pi=pi: E.matmul(
                P[2 * pi + 1], wslabA.ap()[:, oc % 2, dc, :], lnT.ap()[:, dc, 512:1024],
                start=(dc == 0), stop=(dc == ND - 1)),
                "teq" if dc == ND - 1 else None)
        tok_teq[oc] = tok
        b.wait("vector", tok_teq[oc])
        b.drain("vector")
        b.op("vector", lambda E, pi=pi: E.tensor_tensor(
            t1sb.ap(), PA[pi], cosT.ap(), op=ALU.mult))
        b.op("vector", lambda E, pi=pi: E.tensor_tensor(
            t2sb.ap()[0:64, :], PA[pi][64:128, :], sinTs.ap()[0:64, :], op=ALU.mult))
        b.op("vector", lambda E, pi=pi: E.tensor_tensor(
            t2sb.ap()[64:128, :], PA[pi][0:64, :], sinTs.ap()[64:128, :], op=ALU.mult))
        b.drain("vector")
        tok_veq[oc] = b.op("vector", lambda E, oc=oc: E.tensor_tensor(
            qctx.ap()[:, oc, :], t1sb.ap(), t2sb.ap(), op=ALU.add), "veq")
    b.barrier()

    tok_gather = {}
    tok_tek = {}
    tok_vek = {}
    tok_tes = {}
    tok_sce = {}
    tok_tec = {}
    tok_scr = {}
    tok_vec = {}
    tok_vadd = {}
    tok_tesum = {}
    tok_acast = {}

    def gather_kh(kh):
        buf = kh % 2
        if kh >= 2:
            b.wait("gpsimd", tok_tek[kh - 2])
            b.wait("gpsimd", tok_vec[4 * (kh - 2) + 3])
        toks = []
        for ih, btsb in ((0, bt0), (1, bt1)):
            for nm, cache, dst in (("kvK", kcd, Kraw), ("kvV", vcd, Vraw)):
                toks.append(b.dma("gpsimd", lambda E, cache=cache, dst=dst, buf=buf, ih=ih, kh=kh, btsb=btsb:
                                  E.indirect_dma_start(
                                      dst.ap()[:, buf, ih, :], None, cache,
                                      bass.IndirectOffsetOnAxis(btsb.ap(), 0),
                                      element_offset=kh * 2048), f"{nm}{ih}"))
        tok_gather[kh] = toks

    def transpose_kh(kh, extra_wait=None):
        buf = kh % 2
        for t in tok_gather[kh]:
            b.wait("tensor", t)
        if extra_wait is not None:
            b.wait("tensor", extra_wait)
        if kh >= 2:
            b.wait("tensor", tok_vek[kh - 1])
        for c in range(NKC):
            ih, jj = c // 16, c % 16
            tok = b.op("tensor", lambda E, buf=buf, ih=ih, jj=jj, c=c: E.transpose(
                PBF[c // 16][:, (c % 16) * 128:(c % 16 + 1) * 128],
                Kraw.ap()[:, buf, ih, jj * 128:(jj + 1) * 128], idn.ap()),
                "tek" if c == NKC - 1 else None)
        tok_tek[kh] = tok
        b.wait("vector", tok_tek[kh])
        for half in range(2):
            tok = b.op("vector", lambda E, buf=buf, half=half: E.tensor_copy(
                KgT.ap()[:, buf, half * 2048:(half + 1) * 2048], PBF[half]),
                "vek" if half == 1 else None)
        tok_vek[kh] = tok

    SSLOT = (0, 1, 3)

    def emit_scores(gcv, h, kvbuf, c):
        sl = SSLOT[c % 3]
        b.op("tensor", lambda E, kvbuf=kvbuf, c=c, gcv=gcv, h=h, sl=sl: E.matmul(
            P[2 * sl], KgT.ap()[:, kvbuf, c * 128:(c + 1) * 128],
            qctx.ap()[:, h, 0:512], start=True, stop=True))
        tok_tes[gcv] = b.op("tensor", lambda E, kvbuf=kvbuf, c=c, gcv=gcv, h=h, sl=sl: E.matmul(
            P[2 * sl + 1], KgT.ap()[:, kvbuf, c * 128:(c + 1) * 128],
            qctx.ap()[:, h, 512:1024], start=True, stop=True), "tes")

    gather_kh(0)
    transpose_kh(0)
    gather_kh(1)

    gc = 0
    b.wait("tensor", tok_vek[0])
    emit_scores(0, 0, 0, 0)
    emit_scores(1, 0, 0, 1)
    emit_scores(2, 0, 0, 2)
    for kh in range(NKH):
        buf = kh % 2
        for hh in range(4):
            h = 4 * kh + hh
            for c in range(NKC):
                sl = SSLOT[c % 3]
                b.wait("scalar", tok_tes[gc])
                if gc >= 4:
                    b.wait("scalar", tok_tec[gc - 4])
                    b.wait("scalar", tok_vadd[(gc - 4) // 2])
                tok_sce[gc] = b.op("scalar", lambda E, gc=gc, sl=sl: E.activation(
                    expb.ap()[:, gc % 4, :], PA[sl], AF.Exp,
                    bias=zero_ap, scale=invsq_ap), "sce")
                b.wait("tensor", tok_sce[gc])
                if c == 0 and h > 0:
                    b.wait("tensor", tok_vec[h - 1])
                ih, jj = c // 16, c % 16
                b.op("tensor", lambda E, buf=buf, ih=ih, jj=jj, gc=gc, c=c: E.matmul(
                    P[4], Vraw.ap()[:, buf, ih, jj * 128:(jj + 1) * 128],
                    expb.ap()[:, gc % 4, 0:512], start=(c == 0), stop=(c == NKC - 1)))
                tok_tec[gc] = b.op("tensor", lambda E, buf=buf, ih=ih, jj=jj, gc=gc, c=c: E.matmul(
                    P[5], Vraw.ap()[:, buf, ih, jj * 128:(jj + 1) * 128],
                    expb.ap()[:, gc % 4, 512:1024], start=(c == 0), stop=(c == NKC - 1)), "tec")
                if c % 2 == 1:
                    k = c // 2
                    pr = gc // 2
                    b.wait("vector", tok_sce[gc])
                    if k == 0:
                        if h > 0:
                            b.wait("vector", tok_acast[h - 1])
                        tok_vadd[pr] = b.op("vector", lambda E, gc=gc: E.tensor_tensor(
                            sacc.ap(), expb.ap()[:, (gc - 1) % 4, :],
                            expb.ap()[:, gc % 4, :], op=ALU.add), "vadd")
                    else:
                        tok_vadd[pr] = b.op("vector", lambda E, gc=gc, pr=pr: E.tensor_tensor(
                            padd.ap()[:, pr % 2, :], expb.ap()[:, (gc - 1) % 4, :],
                            expb.ap()[:, gc % 4, :], op=ALU.add), "vadd")
                        b.wait("vector", tok_vadd[pr])
                        tok_vadd[pr] = b.op("vector", lambda E, pr=pr: E.tensor_tensor(
                            sacc.ap(), sacc.ap(), padd.ap()[:, pr % 2, :], op=ALU.add), "vadd")
                if c + 3 < NKC:
                    emit_scores(gc + 3, h, buf, c + 3)
                gc += 1
            b.wait("vector", tok_vadd[gc // 2 - 1])
            tok_acast[h] = b.op("vector", lambda E: E.tensor_copy(
                padd.ap()[:, 0, :], sacc.ap()), "acast")
            last_of_kh = (hh == 3)
            if last_of_kh and kh < NKH - 1:
                transpose_kh(kh + 1, extra_wait=tok_sce[gc - 1])
            b.wait("tensor", tok_acast[h])
            if h > 0:
                b.wait("tensor", tok_scr[h - 1])
            b.op("tensor", lambda E: E.matmul(
                P[6][0:1, :], onesc.ap(), padd.ap()[:, 0, 0:512], start=True, stop=True))
            tok_tesum[h] = b.op("tensor", lambda E: E.matmul(
                P[7][0:1, :], onesc.ap(), padd.ap()[:, 0, 512:1024], start=True, stop=True),
                "tesum")
            b.wait("vector", tok_tesum[h])
            b.wait("vector", tok_tec[gc - 1])

            def _recip(E):
                with nc.allow_low_precision(reason="bf16 softmax normalizer"):
                    return E.reciprocal(r_sb.ap(), PA[3][0:1, :])
            tok_ver = b.op("vector", _recip, "ver")
            b.wait("tensor", tok_ver)
            b.op("tensor", lambda E: E.matmul(P[6], onesrb.ap(), r_sb.ap()[:, 0:512],
                                              start=True, stop=True))
            tok_ter = b.op("tensor", lambda E: E.matmul(P[7], onesrb.ap(), r_sb.ap()[:, 512:1024],
                                                        start=True, stop=True), "ter")
            if h < NH - 1:
                if last_of_kh:
                    b.wait("tensor", tok_vek[kh + 1])
                    nbuf = (kh + 1) % 2
                else:
                    nbuf = buf
                emit_scores(gc, h + 1, nbuf, 0)
                emit_scores(gc + 1, h + 1, nbuf, 1)
            b.wait("scalar", tok_ter)
            tok_scr[h] = b.op("scalar", lambda E: E.copy(rb_sb.ap(), PA[3]), "scr")
            b.wait("vector", tok_scr[h])
            tok_vec[h] = b.op("vector", lambda E, h=h: E.tensor_tensor(
                qctx.ap()[:, h, :], PA[2], rb_sb.ap(), op=ALU.mult), "vec")
            if h < NH - 1:
                b.wait("tensor", tok_scr[h])
                emit_scores(gc + 2, h + 1, nbuf, 2)
            if last_of_kh and kh + 2 < NKH:
                gather_kh(kh + 2)
    b.barrier()

    def raw_stats_group(g):
        for tc in range(NT):
            b.op("vector", lambda E, tc=tc, g=g: E.bn_stats(
                STS[tc]["st"].ap()[:, g, :], raw.ap()[:, tc, g * 512:(g + 1) * 512]))

    def raw_stats_final(pfx):
        toks = {}
        b.drain("vector")
        for tc in range(NT):
            toks[tc] = b.op("vector", lambda E, S=STS[tc]: E.bn_aggr(
                S["mv"].ap(), S["st"].ap()[:, :, :]), pfx + "va")
        for tc in range(NT):
            S = STS[tc]
            b.wait("scalar", toks[tc])
            b.drain("scalar")
            b.op("scalar", lambda E, S=S: E.activation(S["rtmp"].ap(), S["mv"].ap()[:, 1:2],
                                                       AF.Ln, bias=eps_ap, scale=one_ap))
            b.drain("scalar")
            b.op("scalar", lambda E, S=S: E.activation(S["rstd"].ap(), S["rtmp"].ap(), AF.Exp,
                                                       bias=zero_ap, scale=neghalf_ap))
            b.drain("scalar")
            toks[tc] = b.op("scalar", lambda E, S=S: E.activation(
                S["mr"].ap(), S["mv"].ap()[:, 0:1], AF.Copy,
                bias=0.0, scale=S["rstd"].ap()), pfx + "sr")
        return toks

    tok_teo = {}
    tok_veo1 = {}
    tok_teo2 = {}
    tok_veo2 = {}

    def p4_transpose(k):
        b.wait("tensor", tok_veo1[k])
        if k >= 2:
            b.wait("tensor", tok_veo2[k - 2])
        for j in range(8):
            tok = b.op("tensor", lambda E, k=k, j=j: E.transpose(
                PBF[3][:, (k % 2) * 1024 + j * 128:(k % 2) * 1024 + (j + 1) * 128],
                aoT.ap()[:, k % 2, j * 128:(j + 1) * 128], idn.ap()),
                "teo2" if j == 7 else None)
        tok_teo2[k] = tok
        b.wait("vector", tok_teo2[k])
        tok_veo2[k] = b.op("vector", lambda E, k=k: E.tensor_copy(
            raw.ap()[:, 0:8, k * 128:(k + 1) * 128],
            PBF[3][:, (k % 2) * 1024:(k % 2) * 1024 + 1024].rearrange("p (a c) -> p a c", a=8)),
            "veo2")

    for oc in range(NH):
        if oc >= 2:
            b.wait("gpsimd", tok_teo[oc - 2])
        tok_slab = b.dma("gpsimd", lambda E, oc=oc: E.dma_start(
            wslabB.ap()[:, oc % 2, :, :], woP[oc].rearrange("p (dc o) -> p dc o", dc=ND)), "wo")
        b.wait("tensor", tok_slab)
        if oc >= 2:
            b.wait("tensor", tok_veo1[oc - 2])
        pi = oc % 2
        for dc in range(ND):
            b.op("tensor", lambda E, oc=oc, dc=dc, pi=pi: E.matmul(
                P[2 * pi], wslabB.ap()[:, oc % 2, dc, :], qctx.ap()[:, dc, 0:512],
                start=(dc == 0), stop=(dc == ND - 1)))
            tok = b.op("tensor", lambda E, oc=oc, dc=dc, pi=pi: E.matmul(
                P[2 * pi + 1], wslabB.ap()[:, oc % 2, dc, :], qctx.ap()[:, dc, 512:1024],
                start=(dc == 0), stop=(dc == ND - 1)),
                "teo" if dc == ND - 1 else None)
        tok_teo[oc] = tok
        b.wait("vector", tok_teo[oc])
        if oc >= 2:
            b.wait("vector", tok_teo2[oc - 2])
        tok_veo1[oc] = b.op("vector", lambda E, oc=oc, pi=pi: E.tensor_copy(
            aoT.ap()[:, oc % 2, :], PA[pi]), "veo1")
        if oc >= 1:
            p4_transpose(oc - 1)
        if oc % 4 == 0 and oc >= 4:
            b.wait("vector", tok_veo2[oc - 1])
            raw_stats_group((oc - 4) // 4)
    p4_transpose(NH - 1)
    b.wait("vector", tok_veo2[NH - 1])
    raw_stats_group(7)
    tok_p5pre = raw_stats_final("p5a")
    b.barrier()

    tok_x1w = {}
    tok_tr5 = {}
    tok_te5 = {}
    tok_p5sc = {}
    tok_scA = {}
    ln2v = [xst.ap()[:, i, :].bitcast(BF16)[:, 0:D] for i in range(2)]
    for it in range(NT + 1):
        if it < NT:
            tc = it
            if tc >= 2:
                b.wait("sync", tok_lnapply[("p5", tc - 2)])
                b.wait("sync", tok_tr5[(tc - 2, 1)])
            tok_x[("p5", tc)] = b.dma("sync", lambda E, tc=tc: E.dma_start(
                xst.ap()[:, tc % 2, :], xr[tc]), "x5l")
        if it >= 1:
            tc = it - 1
            SA = STS[tc]
            b.wait("vector", tok_scA[tc])
            b.wait("vector", tok_x[("p5", tc)])
            if tc >= 1:
                b.wait("vector", tok_x1w[tc - 1])
            b.drain("vector")
            tok_lnapply[("p5", tc)] = b.op("vector", lambda E, SA=SA, tc=tc: E.scalar_tensor_tensor(
                x1t.ap(), lnA.ap(), SA["mr"].ap(), xst.ap()[:, tc % 2, :],
                op0=ALU.subtract, op1=ALU.add), "p5x1")
            b.wait("sync", tok_lnapply[("p5", tc)])
            tok_x1w[tc] = b.dma("sync", lambda E, tc=tc: E.dma_start(x1s[tc], x1t.ap()), "x1w")
            tsc2 = ln_stats(lambda g: x1t.ap()[:, g * 512:(g + 1) * 512], "p5b", 8 + tc % 2)
            b.wait("vector", tsc2)
            if tc >= 1:
                b.wait("vector", tok_tr5[(tc - 1, 1)])
            SB = STS[8 + tc % 2]
            tok_apply2 = b.op("vector", lambda E, SB=SB, tc=tc: E.tensor_scalar(
                ln2v[tc % 2], x1t.ap(), SB["mv"].ap()[:, 0:1], SB["rstd"].ap(),
                op0=ALU.subtract, op1=ALU.mult), "p5ap2")
            transpose_to_lnT(lambda dc, tc=tc: ln2v[tc % 2][:, dc * 128:(dc + 1) * 128],
                             tc, tok_apply2, tok_tr5, tok_te5, "p5")
        if it < NT:
            tc = it
            b.wait("scalar", tok_p5pre[tc])
            if tc >= 1:
                b.wait("scalar", tok_lnapply[("p5", tc - 1)])
            tok_scA[tc] = b.op("scalar", lambda E, tc=tc, S=STS[tc]: E.activation(
                lnA.ap(), raw.ap()[:, tc, :], AF.Copy, bias=0.0, scale=S["rstd"].ap()), "p5sca")
    b.wait("sync", tok_x1w[NT - 2])
    b.wait("sync", tok_x1w[NT - 1])
    b.barrier()

    ta1 = b.dma("gpsimd", lambda E: E.dma_start(a1sb.ap(), a1d), "la1")
    ta3 = b.dma("gpsimd", lambda E: E.dma_start(a3sb.ap(), a3d), "la3")
    ta2 = b.dma("gpsimd", lambda E: E.dma_start(a2sb.ap(), a2d), "la2")
    for name, asb, xasb, tok in (("xa1", a1sb, xa1, ta1), ("xa3", a3sb, xa3, ta3)):
        b.wait("tensor", tok)
        for dc in range(ND):
            b.op("tensor", lambda E, asb=asb, dc=dc: E.matmul(
                P[6][0:16, :], asb.ap()[:, dc, :], lnT.ap()[:, dc, 0:512],
                start=(dc == 0), stop=(dc == ND - 1)))
            tok2 = b.op("tensor", lambda E, asb=asb, dc=dc: E.matmul(
                P[7][0:16, :], asb.ap()[:, dc, :], lnT.ap()[:, dc, 512:1024],
                start=(dc == 0), stop=(dc == ND - 1)),
                "texa" if dc == ND - 1 else None)
        b.wait("vector", tok2)
        tok_xa = b.op("vector", lambda E, xasb=xasb: E.tensor_scalar(
            xasb.ap(), PA[3][0:16, :], two_ap[0:16, :], None, op0=ALU.mult), "vexa")
        b.wait("tensor", tok_xa)

    tok_bst = {}
    tok_teh1 = {}
    tok_teh3 = {}
    tok_scf1 = {}
    tok_vef1 = {}
    tok_texg = {}
    tok_gtw = {}

    def emit_xg2(k):
        b.wait("tensor", tok_vef1[k])
        if k == 0:
            b.wait("tensor", ta2)
        b.op("tensor", lambda E, k=k: E.matmul(
            P[6][0:16, :], a2sb.ap()[:, k, :], gtt.ap()[:, k % 2, 0:512],
            start=(k == 0), stop=(k == NHC - 1)))
        tok_texg[k] = b.op("tensor", lambda E, k=k: E.matmul(
            P[7][0:16, :], a2sb.ap()[:, k, :], gtt.ap()[:, k % 2, 512:1024],
            start=(k == 0), stop=(k == NHC - 1)), "texg")

    for hc in range(NHC):
        grp = hc // 4
        if hc % 4 == 0:
            if grp >= 2:
                b.wait("gpsimd", tok_teh3[4 * (grp - 2) + 3])
            bw = min(512, HID - grp * 512)
            tok1 = b.dma("gpsimd", lambda E, grp=grp, bw=bw: E.dma_start(
                b1st.ap()[:, grp % 2, 0:bw], b1d[:, grp * 512:grp * 512 + bw]), "b1s")
            tok3 = b.dma("gpsimd", lambda E, grp=grp, bw=bw: E.dma_start(
                b3st.ap()[:, grp % 2, 0:bw], b3d[:, grp * 512:grp * 512 + bw]), "b3s")
            tok_bst[grp] = (tok1, tok3)
        if hc >= 2:
            b.wait("gpsimd", tok_teh3[hc - 2])
        tok_w1 = b.dma("gpsimd", lambda E, hc=hc: E.dma_start(
            w1slab.ap()[:, hc % 2, :, :], w1P[hc].rearrange("p (dc o) -> p dc o", dc=ND)), "w1l")
        tok_w3 = b.dma("gpsimd", lambda E, hc=hc: E.dma_start(
            w3slab.ap()[:, hc % 2, :, :], w3P[hc].rearrange("p (dc o) -> p dc o", dc=ND)), "w3l")
        b.wait("tensor", tok_w1)
        b.wait("tensor", tok_bst[grp][0])
        if hc >= 2:
            b.wait("tensor", tok_scf1[hc - 2])
        for dc in range(ND):
            b.op("tensor", lambda E, hc=hc, dc=dc: E.matmul(
                P[0], w1slab.ap()[:, hc % 2, dc, :], lnT.ap()[:, dc, 0:512],
                start=(dc == 0), stop=False))
            b.op("tensor", lambda E, hc=hc, dc=dc: E.matmul(
                P[1], w1slab.ap()[:, hc % 2, dc, :], lnT.ap()[:, dc, 512:1024],
                start=(dc == 0), stop=False))
        b.op("tensor", lambda E, hc=hc, grp=grp: E.matmul(
            P[0], b1st.ap()[:, grp % 2, (hc % 4) * 128:(hc % 4 + 1) * 128],
            xa1.ap()[:, 0:512], start=False, stop=True))
        tok_teh1[hc] = b.op("tensor", lambda E, hc=hc, grp=grp: E.matmul(
            P[1], b1st.ap()[:, grp % 2, (hc % 4) * 128:(hc % 4 + 1) * 128],
            xa1.ap()[:, 512:1024], start=False, stop=True), "teh1")
        if hc >= 1:
            emit_xg2(hc - 1)
        b.wait("tensor", tok_w3)
        b.wait("tensor", tok_bst[grp][1])
        if hc >= 2:
            b.wait("tensor", tok_vef1[hc - 2])
        for dc in range(ND):
            b.op("tensor", lambda E, hc=hc, dc=dc: E.matmul(
                P[2], w3slab.ap()[:, hc % 2, dc, :], lnT.ap()[:, dc, 0:512],
                start=(dc == 0), stop=False))
            b.op("tensor", lambda E, hc=hc, dc=dc: E.matmul(
                P[3], w3slab.ap()[:, hc % 2, dc, :], lnT.ap()[:, dc, 512:1024],
                start=(dc == 0), stop=False))
        b.op("tensor", lambda E, hc=hc, grp=grp: E.matmul(
            P[2], b3st.ap()[:, grp % 2, (hc % 4) * 128:(hc % 4 + 1) * 128],
            xa3.ap()[:, 0:512], start=False, stop=True))
        tok_teh3[hc] = b.op("tensor", lambda E, hc=hc, grp=grp: E.matmul(
            P[3], b3st.ap()[:, grp % 2, (hc % 4) * 128:(hc % 4 + 1) * 128],
            xa3.ap()[:, 512:1024], start=False, stop=True), "teh3")
        b.wait("scalar", tok_teh1[hc])
        if hc >= 1:
            b.wait("scalar", tok_vef1[hc - 1])
        tok_scf1[hc] = b.op("scalar", lambda E: E.activation(
            s1.ap(), PA[0], AF.Silu, bias=zero_ap, scale=one_ap), "scf1")
        b.wait("vector", tok_teh3[hc])
        b.wait("vector", tok_scf1[hc])
        if hc >= 2:
            b.wait("vector", tok_texg[hc - 2])
            b.wait("vector", tok_gtw[hc - 2])
        tok_vef1[hc] = b.op("vector", lambda E, hc=hc: E.tensor_tensor(
            gtt.ap()[:, hc % 2, :], PA[1], s1.ap(), op=ALU.mult), "vef1")
        b.wait("sync", tok_vef1[hc])
        tok_gtw[hc] = b.dma("sync", lambda E, hc=hc: E.dma_start(gts[hc], gtt.ap()[:, hc % 2, :]), "gtw")
    emit_xg2(NHC - 1)
    b.wait("vector", tok_texg[NHC - 1])
    tok_vexg = b.op("vector", lambda E: E.tensor_scalar(
        xg2.ap(), PA[3][0:16, :], two_ap[0:16, :], None, op0=ALU.mult), "vexg")
    b.wait("sync", tok_gtw[NHC - 2])
    b.wait("sync", tok_gtw[NHC - 1])
    b.barrier()
    tok_b2 = b.dma("gpsimd", lambda E: E.dma_start(b2sb.ap(), b2d), "b2l")

    tok_w2g = {}
    tok_gtr = {}
    tok_tew2c = {}
    tok_teog = {}
    tok_vew1 = {}
    tok_tew2t = {}
    tok_vew2 = {}

    WGEND = {0: 2, 1: 5, 2: 8, 3: 10}
    NW = len(W2GRP)
    NG = len(GTGRP)
    oc_slots = [(ogi, j) for ogi, (oc0, ow) in enumerate(OGS) for j in range(ow)]
    tew2c_val = lambda ogi, gg: ogi * NG + gg + 1

    for ogi, (oc0, ow) in enumerate(OGS):
        for wg, (h0, hw) in enumerate(W2GRP):
            gslab = ogi * NW + wg
            if gslab >= 2:
                pogi, pwg = divmod(gslab - 2, NW)
                b.wait("gpsimd", ("tew2c", tew2c_val(pogi, WGEND[pwg])))
            tok_w2g[(ogi, wg)] = b.dma("gpsimd", lambda E, h0=h0, hw=hw, oc0=oc0, ow=ow, wg=wg:
                                       E.dma_start(w2slab.ap()[:, wg % 2, 0:hw, 0:ow * 128],
                                                   w2P[:, h0:h0 + hw, oc0 * 128:oc0 * 128 + ow * 128]),
                                       "w2s")
        for gg, (g0, gw) in enumerate(GTGRP):
            ggl = ogi * NG + gg
            if ggl >= 2:
                pogi, pgg = divmod(ggl - 2, NG)
                b.wait("sync", ("tew2c", tew2c_val(pogi, pgg)))
            tok_gtr[(ogi, gg)] = b.dma("sync", lambda E, g0=g0, gw=gw, bufg=(ogi * NG + gg) % 2:
                                       E.dma_start(gt_rd.ap()[:, bufg, 0:gw, :],
                                                   gts[g0:g0 + gw].rearrange("g p t -> p g t")), "gtr")
        if ogi > 0:
            b.wait("tensor", tok_vew1[(ogi - 1, OGS[ogi - 1][1] - 1)])
        for hc in range(NHC):
            wg = next(i for i, (h0, hw) in enumerate(W2GRP) if h0 <= hc < h0 + hw)
            gg = hc // 8
            if hc == W2GRP[wg][0]:
                b.wait("tensor", tok_w2g[(ogi, wg)])
            if hc % 8 == 0:
                b.wait("tensor", tok_gtr[(ogi, gg)])
            for j in range(ow):
                lastg = (hc % 8 == 7 or hc == NHC - 1) and j == ow - 1
                bufg = (ogi * NG + gg) % 2
                b.op("tensor", lambda E, hc=hc, wg=wg, bufg=bufg, j=j: E.matmul(
                    P[2 * j], w2slab.ap()[:, wg % 2, hc - W2GRP[wg][0], j * 128:(j + 1) * 128],
                    gt_rd.ap()[:, bufg, hc % 8, 0:512], start=(hc == 0), stop=False))
                tok = b.op("tensor", lambda E, hc=hc, wg=wg, bufg=bufg, j=j: E.matmul(
                    P[2 * j + 1], w2slab.ap()[:, wg % 2, hc - W2GRP[wg][0], j * 128:(j + 1) * 128],
                    gt_rd.ap()[:, bufg, hc % 8, 512:1024], start=(hc == 0), stop=False),
                    ("tew2c" if lastg else None))
                if tok is not None:
                    tok_tew2c[(ogi, gg)] = tok
                    assert tok[1] == tew2c_val(ogi, gg), (tok, ogi, gg)
        b.wait("tensor", tok_vexg)
        b.wait("tensor", tok_b2)
        for j in range(ow):
            oc = oc0 + j
            b.op("tensor", lambda E, oc=oc, j=j: E.matmul(
                P[2 * j], b2sb.ap()[:, oc * 128:(oc + 1) * 128], xg2.ap()[:, 0:512],
                start=False, stop=True))
            tok = b.op("tensor", lambda E, oc=oc, j=j: E.matmul(
                P[2 * j + 1], b2sb.ap()[:, oc * 128:(oc + 1) * 128], xg2.ap()[:, 512:1024],
                start=False, stop=True), "teog" if j == ow - 1 else None)
        tok_teog[ogi] = tok
        for j in range(ow):
            oc = oc0 + j
            slot = oc_slots.index((ogi, j))
            b.wait("vector", tok_teog[ogi])
            prevsame = None
            for s in range(slot - 1, -1, -1):
                if oc_slots[s][1] % 2 == j % 2:
                    prevsame = oc_slots[s]
                    break
            if prevsame is not None:
                b.wait("vector", tok_tew2t[prevsame])
            tok_vew1[(ogi, j)] = b.op("vector", lambda E, j=j: E.tensor_copy(
                oTsb.ap()[:, j % 2, :], PA[j]), "vew1")
            b.wait("tensor", tok_vew1[(ogi, j)])
            if slot >= 2:
                b.wait("tensor", tok_vew2[oc_slots[slot - 2]])
            half = slot % 2
            for jj in range(8):
                tok = b.op("tensor", lambda E, j=j, jj=jj, half=half: E.transpose(
                    PBF[3][:, half * 1024 + jj * 128:half * 1024 + (jj + 1) * 128],
                    oTsb.ap()[:, j % 2, jj * 128:(jj + 1) * 128], idn.ap()),
                    "tew2t" if jj == 7 else None)
            tok_tew2t[(ogi, j)] = tok
            b.wait("vector", tok_tew2t[(ogi, j)])
            tok_vew2[(ogi, j)] = b.op("vector", lambda E, oc=oc, half=half: E.tensor_copy(
                raw.ap()[:, 0:8, oc * 128:(oc + 1) * 128],
                PBF[3][:, half * 1024:half * 1024 + 1024].rearrange("p (a c) -> p a c", a=8)),
                "vew2")
            if oc % 4 == 3:
                b.wait("vector", tok_vew2[(ogi, j)])
                raw_stats_group(oc // 4)
    tok_f3pre = raw_stats_final("f3a")
    b.barrier()

    tok_x1r = {}
    tok_yw = {}
    tok_out = {}
    tok_f3sc = {}
    tok_scF = {}
    for it in range(NT + 1):
        if it < NT:
            tc = it
            if tc >= 2:
                b.wait("sync", tok_out[tc - 2])
            tok_x1r[tc] = b.dma("sync", lambda E, tc=tc: E.dma_start(
                x1rd.ap()[:, tc % 2, :], x1s[tc]), "x1r")
        if it >= 1:
            tc = it - 1
            S = STS[tc]
            b.wait("vector", tok_scF[tc])
            b.wait("vector", tok_x1r[tc])
            if tc >= 1:
                b.wait("vector", tok_yw[tc - 1])
            b.drain("vector")
            tok_out[tc] = b.op("vector", lambda E, S=S, tc=tc: E.scalar_tensor_tensor(
                outT.ap(), lnF.ap(), S["mr"].ap(), x1rd.ap()[:, tc % 2, :],
                op0=ALU.subtract, op1=ALU.add), "veout")
            b.wait("sync", tok_out[tc])
            tok_yw[tc] = b.dma("sync", lambda E, tc=tc: E.dma_start(yr[tc], outT.ap()), "yw")
        if it < NT:
            tc = it
            b.wait("scalar", tok_f3pre[tc])
            if tc >= 1:
                b.wait("scalar", tok_out[tc - 1])
            tok_scF[tc] = b.op("scalar", lambda E, tc=tc, S=STS[tc]: E.activation(
                lnF.ap(), raw.ap()[:, tc, :], AF.Copy, bias=0.0, scale=S["rstd"].ap()), "f3sca")
    b.wait("sync", tok_yw[NT - 1])
    b.check()

    with nc.Block() as block:
        @block.sync
        def _(E):
            for f in b.ops["sync"]:
                f(E)

        @block.gpsimd
        def _(E):
            for f in b.ops["gpsimd"]:
                f(E)

        @block.tensor
        def _(E):
            for f in b.ops["tensor"]:
                f(E)

        @block.scalar
        def _(E):
            for f in b.ops["scalar"]:
                f(E)

        @block.vector
        def _(E):
            for f in b.ops["vector"]:
                f(E)

    stack.close()
    return nc


def kernel(L0_attn_hidden_states, L0_attn_k_cache, L0_attn_v_cache, L0_attn_Wq, L0_attn_Wo,
           L0_attn_block_table, L0_ffn_w1, L0_ffn_w3, L0_ffn_w2, L0_ffn_lora_A1, L0_ffn_lora_B1,
           L0_ffn_lora_A3, L0_ffn_lora_B3, L0_ffn_lora_A2, L0_ffn_lora_B2):
    x = np.ascontiguousarray(np.asarray(L0_attn_hidden_states, np.float32).reshape(8192, D))
    Wq = np.asarray(L0_attn_Wq, np.float32)
    Wo = np.asarray(L0_attn_Wo, np.float32)
    w1 = np.asarray(L0_ffn_w1, np.float32)
    w3 = np.asarray(L0_ffn_w3, np.float32)
    w2 = np.asarray(L0_ffn_w2, np.float32)

    wqP = np.ascontiguousarray(Wq.reshape(32, 128, ND, 128).transpose(0, 3, 2, 1)
                               .reshape(32, 128, ND * 128))
    woP = np.ascontiguousarray(Wo.reshape(32, 128, ND, 128).transpose(0, 3, 2, 1)
                               .reshape(32, 128, ND * 128))
    w1P = np.ascontiguousarray(w1.reshape(NHC, 128, ND, 128).transpose(0, 3, 2, 1)
                               .reshape(NHC, 128, ND * 128))
    w3P = np.ascontiguousarray(w3.reshape(NHC, 128, ND, 128).transpose(0, 3, 2, 1)
                               .reshape(NHC, 128, ND * 128))
    w2P = np.ascontiguousarray(w2.T.reshape(NHC, 128, D).transpose(1, 0, 2))
    a1P = np.ascontiguousarray(np.asarray(L0_ffn_lora_A1, np.float32).T
                               .reshape(ND, 128, 16).transpose(1, 0, 2))
    a3P = np.ascontiguousarray(np.asarray(L0_ffn_lora_A3, np.float32).T
                               .reshape(ND, 128, 16).transpose(1, 0, 2))
    a2P = np.ascontiguousarray(np.asarray(L0_ffn_lora_A2, np.float32).T
                               .reshape(NHC, 128, 16).transpose(1, 0, 2))
    b1T = np.ascontiguousarray(np.asarray(L0_ffn_lora_B1, np.float32).T)
    b3T = np.ascontiguousarray(np.asarray(L0_ffn_lora_B3, np.float32).T)
    b2T = np.ascontiguousarray(np.asarray(L0_ffn_lora_B2, np.float32).T)
    kc = np.ascontiguousarray(np.asarray(L0_attn_k_cache, np.float32).reshape(512, 16384))
    vc = np.ascontiguousarray(np.asarray(L0_attn_v_cache, np.float32).reshape(512, 16384))
    bt = np.ascontiguousarray(np.asarray(L0_attn_block_table, np.int32).reshape(2, 128))

    consts = np.zeros((128, 8), np.float32)
    consts[:, 0] = 1.0
    consts[:, 1] = EPS
    consts[:, 2] = INVSQ
    consts[:, 3] = -0.5
    consts[:, 4] = 2.0
    consts[:, 5] = 0.0
    idn = np.eye(128, dtype=np.float32)
    onesr = np.ones((1, 128), np.float32)

    invf = (1.0 / (10000.0 ** (np.arange(0, 128, 2, dtype=np.float32) / 128.0))).astype(np.float32)

    nc = build_program()
    in_maps = []
    for c in range(8):
        s0 = (c % 4) * 1024
        pos = np.arange(s0, s0 + 1024, dtype=np.float32)
        ang = np.concatenate([np.outer(invf, pos), np.outer(invf, pos)], 0)
        cosT = np.cos(ang).astype(np.float32)
        sinT = np.sin(ang).astype(np.float32)
        sinTs = np.concatenate([-sinT[0:64], sinT[64:128]], 0).astype(np.float32)
        m = {
            "x": np.ascontiguousarray(x[c * 1024:(c + 1) * 1024]),
            "wqP": wqP, "woP": woP, "w1P": w1P, "w3P": w3P, "w2P": w2P,
            "a1P": a1P, "a3P": a3P, "a2P": a2P, "b1T": b1T, "b3T": b3T, "b2T": b2T,
            "kc": kc, "vc": vc, "bt": bt,
            "cosT": np.ascontiguousarray(cosT), "sinTs": np.ascontiguousarray(sinTs),
            "idn": idn, "consts": consts, "onesr": onesr,
        }
        in_maps.append(m)
    res = run_bass_kernel_spmd(nc, in_maps, core_ids=list(range(8)))
    y = np.concatenate([np.asarray(r["y"], np.float32) for r in res.results], 0)
    return y.reshape(2, 4096, D)


# version 18 (final)
# speedup vs baseline: 1.0768x; 1.0768x over previous
import contextlib
import numpy as np
import concourse.bass as bass
import concourse.mybir as mybir
from concourse.bass_utils import run_bass_kernel_spmd

F32 = mybir.dt.float32
BF16 = mybir.dt.bfloat16
I32 = mybir.dt.int32
AF = mybir.ActivationFunctionType
ALU = mybir.AluOpType

D = 4096
T = 1024
NT = 8
ND = 32
NH = 32
NKH = 8
HID = 11008
NHC = 86
KV = 4096
NKC = 32
INVSQ = 1.0 / np.sqrt(128.0)
EPS = 1e-5

ENGS = ("sync", "gpsimd", "tensor", "scalar", "vector")

OGS = [(i, min(3, 32 - i)) for i in range(0, 32, 3)]
W2GRP = [(0, 22), (22, 22), (44, 22), (66, 20)]
GTGRP = [(i, min(8, NHC - i)) for i in range(0, NHC, 8)]


class Bld:
    def __init__(self, nc, stack):
        self.nc = nc
        self.ops = {e: [] for e in ENGS}
        self.sems = {}
        self.stack = stack
        self.dmaseq = {}

    def _sem(self, name):
        if name not in self.sems:
            h = self.stack.enter_context(self.nc.semaphore(name))
            self.sems[name] = [h, 0, 0]
        return self.sems[name]

    def op(self, eng, fn, inc=None, k=1):
        if inc is None:
            self.ops[eng].append(fn)
            return None
        rec = self._sem(inc)
        rec[1] += k
        tgt = rec[1]
        h = rec[0]

        def f(E, fn=fn, h=h, k=k):
            fn(E).then_inc(h, k)
        self.ops[eng].append(f)
        return (inc, tgt)

    def dma(self, eng, fn, inc, parity=True, nway=2):
        if parity:
            seq = self.dmaseq.get(inc, 0)
            self.dmaseq[inc] = seq + 1
            inc = f"{inc}{seq % nway}"
        return self.op(eng, fn, inc, 16)

    def wait(self, eng, tok):
        if tok is None:
            return
        name, v = tok
        rec = self._sem(name)
        rec[2] = max(rec[2], v)
        h = rec[0]
        self.ops[eng].append(lambda E, h=h, v=v: E.wait_ge(h, v))

    def drain(self, eng):
        self.ops[eng].append(lambda E: E.drain())

    def barrier(self):
        rec = self._sem("ph")
        tgt = rec[1] + len(ENGS)
        h = rec[0]
        for e in ENGS:
            self.ops[e].append(lambda E: E.drain())
            self.ops[e].append(lambda E, h=h: E.sem_inc(h, 1))
            self.ops[e].append(lambda E, h=h, v=tgt: E.wait_ge(h, v))
        rec[1] = tgt
        rec[2] = max(rec[2], tgt)

    def check(self):
        for name, (h, cnt, maxw) in self.sems.items():
            assert maxw <= cnt, f"sem {name}: waited {maxw} > produced {cnt}"


def build_program():
    nc = bass.Bass("TRN2", target_bir_lowering=False)

    xd = nc.dram_tensor("x", [T, D], F32, kind="ExternalInput").ap()
    yd = nc.dram_tensor("y", [T, D], F32, kind="ExternalOutput").ap()
    wqP = nc.dram_tensor("wqP", [32, 128, ND * 128], F32, kind="ExternalInput").ap()
    woP = nc.dram_tensor("woP", [32, 128, ND * 128], F32, kind="ExternalInput").ap()
    w1P = nc.dram_tensor("w1P", [NHC, 128, ND * 128], F32, kind="ExternalInput").ap()
    w3P = nc.dram_tensor("w3P", [NHC, 128, ND * 128], F32, kind="ExternalInput").ap()
    w2P = nc.dram_tensor("w2P", [128, NHC, D], F32, kind="ExternalInput").ap()
    a1d = nc.dram_tensor("a1P", [128, ND, 16], F32, kind="ExternalInput").ap()
    a3d = nc.dram_tensor("a3P", [128, ND, 16], F32, kind="ExternalInput").ap()
    a2d = nc.dram_tensor("a2P", [128, NHC, 16], F32, kind="ExternalInput").ap()
    b1d = nc.dram_tensor("b1T", [16, HID], F32, kind="ExternalInput").ap()
    b3d = nc.dram_tensor("b3T", [16, HID], F32, kind="ExternalInput").ap()
    b2d = nc.dram_tensor("b2T", [16, D], F32, kind="ExternalInput").ap()
    kcd = nc.dram_tensor("kc", [512, 16384], F32, kind="ExternalInput").ap()
    vcd = nc.dram_tensor("vc", [512, 16384], F32, kind="ExternalInput").ap()
    btd = nc.dram_tensor("bt", [2, 128], I32, kind="ExternalInput").ap()
    cosd = nc.dram_tensor("cosT", [128, T], F32, kind="ExternalInput").ap()
    sind = nc.dram_tensor("sinTs", [128, T], F32, kind="ExternalInput").ap()
    idnd = nc.dram_tensor("idn", [128, 128], F32, kind="ExternalInput").ap()
    constd = nc.dram_tensor("consts", [128, 8], F32, kind="ExternalInput").ap()
    onesrd = nc.dram_tensor("onesr", [1, 128], F32, kind="ExternalInput").ap()

    gts = nc.dram_tensor("gts", [NHC, 128, T], BF16).ap()
    x1s = nc.dram_tensor("x1s", [NT, 128, D], F32).ap()

    A_OFF = 32736
    B_OFF = A_OFF + 65536
    C_OFF = B_OFF + 65536
    assert C_OFF + 65536 <= 229344

    al = nc.alloc_sbuf_tensor_at
    lnT = al("lnT", [128, ND, T], BF16, offset=A_OFF)
    Kraw = al("Kraw", [128, 2, 2, 2048], BF16, offset=A_OFF)
    Vraw = al("Vraw", [128, 2, 2, 2048], BF16, offset=A_OFF + 16384)
    KgT = al("KgT", [128, 2, KV], BF16, offset=A_OFF + 32768)
    expb = al("expb", [128, 4, T], BF16, offset=A_OFF + 49152)
    padd = al("padd", [128, 2, T], BF16, offset=A_OFF + 57344)
    rb_sb = al("rb_sb", [128, T], BF16, offset=A_OFF + 61440)
    r_sb = al("r_sb", [1, T], BF16, offset=A_OFF + 63488)
    wslabB = al("wslabB", [128, 2, ND, 128], BF16, offset=A_OFF)
    aoT = al("aoT", [128, 2, T], BF16, offset=A_OFF + 16384)
    gt_rd = al("gt_rd", [128, 2, 8, T], BF16, offset=A_OFF)
    b2sb = al("b2sb", [16, D], BF16, offset=A_OFF + 32768)
    qctx = al("qctx", [128, NH, T], BF16, offset=B_OFF)
    xst = al("xst", [128, 3, D], F32, offset=B_OFF)
    ln1t = al("ln1t", [128, 2, D], BF16, offset=B_OFF + 49152)
    lnA = al("lnA", [128, D], F32, offset=B_OFF + 49152)
    x1t = al("x1t", [128, D], F32, offset=B_OFF + 32768)
    s1 = al("s1", [128, T], F32, offset=B_OFF)
    gtt = al("gtt", [128, 2, T], BF16, offset=B_OFF + 4096)
    w2slab = al("w2slab", [128, 2, 22, 384], BF16, offset=B_OFF)
    oTsb = al("oTsb", [128, 2, T], BF16, offset=B_OFF + 33792)
    x1rd = al("x1rd", [128, 2, D], F32, offset=B_OFF)
    lnF = al("lnF", [128, D], F32, offset=B_OFF + 32768)
    outT = al("outT", [128, D], F32, offset=B_OFF + 49152)
    raw = al("raw", [128, NT, D], BF16, offset=C_OFF)
    cosT = al("cosT", [128, T], F32, offset=C_OFF)
    sinTs = al("sinTs", [128, T], F32, offset=C_OFF + 4096)
    wslabA = al("wslabA", [128, 2, ND, 128], BF16, offset=C_OFF + 8192)
    t1sb = al("t1sb", [128, T], F32, offset=C_OFF + 24576)
    t2sb = al("t2sb", [128, T], F32, offset=C_OFF + 28672)
    w1slab = al("w1slab", [128, 2, ND, 128], BF16, offset=C_OFF)
    w3slab = al("w3slab", [128, 2, ND, 128], BF16, offset=C_OFF + 16384)
    a1sb = al("a1sb", [128, ND, 16], BF16, offset=C_OFF + 32768)
    a3sb = al("a3sb", [128, ND, 16], BF16, offset=C_OFF + 33792)
    a2sb = al("a2sb", [128, NHC, 16], BF16, offset=C_OFF + 34816)

    consts = nc.alloc_sbuf_tensor("consts_sb", [128, 8], F32)
    idn = nc.alloc_sbuf_tensor("idn_sb", [128, 128], BF16)
    bt0 = nc.alloc_sbuf_tensor("bt0", [128, 1], I32)
    bt1 = nc.alloc_sbuf_tensor("bt1", [128, 1], I32)
    onesr = nc.alloc_sbuf_tensor("onesr_sb", [1, 128], F32)
    onesrb = nc.alloc_sbuf_tensor("onesrb", [1, 128], BF16)
    onesc = nc.alloc_sbuf_tensor("onesc", [128, 1], BF16)
    STS = []
    for i in range(10):
        STS.append(dict(
            st=nc.alloc_sbuf_tensor(f"st{i}", [128, 8, 6], F32),
            mv=nc.alloc_sbuf_tensor(f"mv{i}", [128, 2], F32),
            rstd=nc.alloc_sbuf_tensor(f"rstd{i}", [128, 1], F32),
            rtmp=nc.alloc_sbuf_tensor(f"rtmp{i}", [128, 1], F32),
            mr=nc.alloc_sbuf_tensor(f"mr{i}", [128, 1], F32),
        ))
    xg2 = nc.alloc_sbuf_tensor("xg2", [16, T], BF16)
    sacc = nc.alloc_sbuf_tensor("sacc", [128, T], F32)
    xa1 = al("xa1", [16, T], BF16, offset=C_OFF + 37888)
    xa3 = al("xa3", [16, T], BF16, offset=C_OFF + 39936)
    b1st = al("b1st", [16, 2, 512], BF16, offset=C_OFF + 41984)
    b3st = al("b3st", [16, 2, 512], BF16, offset=C_OFF + 44032)
    assert nc.sbuf_base <= A_OFF, f"bump overflow: {nc.sbuf_base} > {A_OFF}"

    one_ap = consts.ap()[:, 0:1]
    eps_ap = consts.ap()[:, 1:2]
    invsq_ap = consts.ap()[:, 2:3]
    neghalf_ap = consts.ap()[:, 3:4]
    two_ap = consts.ap()[:, 4:5]
    zero_ap = consts.ap()[:, 5:6]

    PP = [nc.alloc_psum_tensor(f"PP{i}", [128, 1024], F32) for i in range(4)]
    PA = [pp.ap() for pp in PP]
    P = []
    for i in range(4):
        P.append(PA[i][:, 0:512])
        P.append(PA[i][:, 512:1024])
    PBF = [PA[i].bitcast(BF16) for i in range(4)]

    stack = contextlib.ExitStack()
    b = Bld(nc, stack)

    xr = xd.rearrange("(tc p) d -> tc p d", p=128)
    yr = yd.rearrange("(tc p) d -> tc p d", p=128)

    for dst, src in ((consts.ap(), constd), (idn.ap(), idnd),
                     (bt0.ap(), btd[0].rearrange("(p one) -> p one", one=1)),
                     (bt1.ap(), btd[1].rearrange("(p one) -> p one", one=1)),
                     (onesr.ap(), onesrd), (onesc.ap(), onesrd.rearrange("one p -> p one")),
                     (onesrb.ap(), onesrd),
                     (cosT.ap(), cosd), (sinTs.ap(), sind)):
        b.dma("gpsimd", lambda E, dst=dst, src=src: E.dma_start(dst, src), "ld", parity=False)
    b.wait("gpsimd", ("ld", 9 * 16))
    b.barrier()

    def ln_stats(src_ap_fn, sem_pfx, si):
        S = STS[si]
        b.drain("vector")
        for g in range(8):
            b.op("vector", lambda E, g=g, f=src_ap_fn, S=S: E.bn_stats(S["st"].ap()[:, g, :], f(g)))
        b.drain("vector")
        tva = b.op("vector", lambda E, S=S: E.bn_aggr(S["mv"].ap(), S["st"].ap()[:, :, :]),
                   sem_pfx + "va")
        b.wait("scalar", tva)
        b.drain("scalar")
        b.op("scalar", lambda E, S=S: E.activation(S["rtmp"].ap(), S["mv"].ap()[:, 1:2], AF.Ln,
                                                   bias=eps_ap, scale=one_ap))
        b.drain("scalar")
        b.op("scalar", lambda E, S=S: E.activation(S["rstd"].ap(), S["rtmp"].ap(), AF.Exp,
                                                   bias=zero_ap, scale=neghalf_ap))
        b.drain("scalar")
        tsc = b.op("scalar", lambda E, S=S: E.activation(S["mr"].ap(), S["mv"].ap()[:, 0:1], AF.Copy,
                                                         bias=0.0, scale=S["rstd"].ap()), sem_pfx + "sr")
        return tsc

    def transpose_to_lnT(src_ap_fn, tc, tok_ready, tok_grp, tok_ev, pfx):
        for g in range(2):
            b.wait("tensor", tok_ready)
            if (tc, g) != (0, 0):
                prev = (tc, g - 1) if g > 0 else (tc - 1, 1)
                b.wait("tensor", tok_ev[prev])
            for j in range(16):
                dc = g * 16 + j
                tok = b.op("tensor", lambda E, dc=dc, j=j, f=src_ap_fn: E.transpose(
                    PBF[3][:, j * 128:(j + 1) * 128], f(dc), idn.ap()),
                    pfx + "t" if j == 15 else None)
            tok_grp[(tc, g)] = tok
            b.wait("vector", tok_grp[(tc, g)])
            tok_ev[(tc, g)] = b.op("vector", lambda E, tc=tc, g=g: E.tensor_copy(
                lnT.ap()[:, g * 16:(g + 1) * 16, tc * 128:(tc + 1) * 128],
                PBF[3].rearrange("p (a c) -> p a c", a=16)), pfx + "e")

    tok_x = {}
    tok_lnapply = {}
    tok_trgrp = {}
    tok_trev = {}
    tok_p1sc = {}
    for it in range(NT + 2):
        if it < NT:
            tc = it
            if tc >= 3:
                b.wait("sync", tok_lnapply.get(tc - 3))
            tok_x[tc] = b.dma("sync", lambda E, tc=tc: E.dma_start(xst.ap()[:, tc % 3, :], xr[tc]),
                              "xin", nway=3)
            b.wait("vector", tok_x[tc])
            tok_p1sc[tc] = ln_stats(lambda g, tc=tc: xst.ap()[:, tc % 3, g * 512:(g + 1) * 512],
                                    "p1", tc % 3)
        if it >= 2:
            tc = it - 2
            S = STS[tc % 3]
            b.wait("vector", tok_p1sc[tc])
            if tc >= 2:
                b.wait("vector", tok_trgrp[(tc - 2, 1)])
            tok_lnapply[tc] = b.op("vector", lambda E, tc=tc, S=S: E.tensor_scalar(
                ln1t.ap()[:, tc % 2, :], xst.ap()[:, tc % 3, :], S["mv"].ap()[:, 0:1],
                S["rstd"].ap(), op0=ALU.subtract, op1=ALU.mult), "p1ap")
            transpose_to_lnT(lambda dc, tc=tc: ln1t.ap()[:, tc % 2, dc * 128:(dc + 1) * 128],
                             tc, tok_lnapply[tc], tok_trgrp, tok_trev, "p1")
    tok_wq_pre = {}
    for oc in range(2):
        tok_wq_pre[oc] = b.dma("gpsimd", lambda E, oc=oc: E.dma_start(
            wslabA.ap()[:, oc % 2, :, :], wqP[oc].rearrange("p (dc o) -> p dc o", dc=ND)), "wq")
    b.barrier()

    tok_teq = {}
    tok_veq = {}
    for oc in range(NH):
        if oc < 2:
            tok_slab = tok_wq_pre[oc]
        else:
            b.wait("gpsimd", tok_teq[oc - 2])
            tok_slab = b.dma("gpsimd", lambda E, oc=oc: E.dma_start(
                wslabA.ap()[:, oc % 2, :, :], wqP[oc].rearrange("p (dc o) -> p dc o", dc=ND)), "wq")
        b.wait("tensor", tok_slab)
        if oc >= 2:
            b.wait("tensor", tok_veq[oc - 2])
        pi = oc % 2
        for dc in range(ND):
            b.op("tensor", lambda E, oc=oc, dc=dc, pi=pi: E.matmul(
                P[2 * pi], wslabA.ap()[:, oc % 2, dc, :], lnT.ap()[:, dc, 0:512],
                start=(dc == 0), stop=(dc == ND - 1)))
            tok = b.op("tensor", lambda E, oc=oc, dc=dc, pi=pi: E.matmul(
                P[2 * pi + 1], wslabA.ap()[:, oc % 2, dc, :], lnT.ap()[:, dc, 512:1024],
                start=(dc == 0), stop=(dc == ND - 1)),
                "teq" if dc == ND - 1 else None)
        tok_teq[oc] = tok
        b.wait("vector", tok_teq[oc])
        b.drain("vector")
        b.op("vector", lambda E, pi=pi: E.tensor_tensor(
            t1sb.ap(), PA[pi], cosT.ap(), op=ALU.mult))
        b.op("vector", lambda E, pi=pi: E.tensor_tensor(
            t2sb.ap()[0:64, :], PA[pi][64:128, :], sinTs.ap()[0:64, :], op=ALU.mult))
        b.op("vector", lambda E, pi=pi: E.tensor_tensor(
            t2sb.ap()[64:128, :], PA[pi][0:64, :], sinTs.ap()[64:128, :], op=ALU.mult))
        b.drain("vector")
        tok_veq[oc] = b.op("vector", lambda E, oc=oc: E.tensor_tensor(
            qctx.ap()[:, oc, :], t1sb.ap(), t2sb.ap(), op=ALU.add), "veq")
    b.barrier()

    tok_gather = {}
    tok_tek = {}
    tok_vek = {}
    tok_tes = {}
    tok_sce = {}
    tok_tec = {}
    tok_scr = {}
    tok_vec = {}
    tok_vadd = {}
    tok_tesum = {}
    tok_acast = {}

    def gather_kh(kh):
        buf = kh % 2
        if kh >= 2:
            b.wait("gpsimd", tok_tek[kh - 2])
            b.wait("gpsimd", tok_vec[4 * (kh - 2) + 3])
        toks = []
        for ih, btsb in ((0, bt0), (1, bt1)):
            for nm, cache, dst in (("kvK", kcd, Kraw), ("kvV", vcd, Vraw)):
                toks.append(b.dma("gpsimd", lambda E, cache=cache, dst=dst, buf=buf, ih=ih, kh=kh, btsb=btsb:
                                  E.indirect_dma_start(
                                      dst.ap()[:, buf, ih, :], None, cache,
                                      bass.IndirectOffsetOnAxis(btsb.ap(), 0),
                                      element_offset=kh * 2048), f"{nm}{ih}"))
        tok_gather[kh] = toks

    def transpose_kh(kh, extra_wait=None):
        buf = kh % 2
        for t in tok_gather[kh]:
            b.wait("tensor", t)
        if extra_wait is not None:
            b.wait("tensor", extra_wait)
        if kh >= 2:
            b.wait("tensor", tok_vek[kh - 1])
        for c in range(NKC):
            ih, jj = c // 16, c % 16
            tok = b.op("tensor", lambda E, buf=buf, ih=ih, jj=jj, c=c: E.transpose(
                PBF[c // 16][:, (c % 16) * 128:(c % 16 + 1) * 128],
                Kraw.ap()[:, buf, ih, jj * 128:(jj + 1) * 128], idn.ap()),
                "tek" if c == NKC - 1 else None)
        tok_tek[kh] = tok
        b.wait("vector", tok_tek[kh])
        for half in range(2):
            tok = b.op("vector", lambda E, buf=buf, half=half: E.tensor_copy(
                KgT.ap()[:, buf, half * 2048:(half + 1) * 2048], PBF[half]),
                "vek" if half == 1 else None)
        tok_vek[kh] = tok

    SSLOT = (0, 1, 3)

    def emit_scores(gcv, h, kvbuf, c):
        sl = SSLOT[c % 3]
        b.op("tensor", lambda E, kvbuf=kvbuf, c=c, gcv=gcv, h=h, sl=sl: E.matmul(
            P[2 * sl], KgT.ap()[:, kvbuf, c * 128:(c + 1) * 128],
            qctx.ap()[:, h, 0:512], start=True, stop=True))
        tok_tes[gcv] = b.op("tensor", lambda E, kvbuf=kvbuf, c=c, gcv=gcv, h=h, sl=sl: E.matmul(
            P[2 * sl + 1], KgT.ap()[:, kvbuf, c * 128:(c + 1) * 128],
            qctx.ap()[:, h, 512:1024], start=True, stop=True), "tes")

    gather_kh(0)
    transpose_kh(0)
    gather_kh(1)

    gc = 0
    b.wait("tensor", tok_vek[0])
    emit_scores(0, 0, 0, 0)
    emit_scores(1, 0, 0, 1)
    emit_scores(2, 0, 0, 2)
    for kh in range(NKH):
        buf = kh % 2
        for hh in range(4):
            h = 4 * kh + hh
            for c in range(NKC):
                sl = SSLOT[c % 3]
                b.wait("scalar", tok_tes[gc])
                if gc >= 4:
                    b.wait("scalar", tok_tec[gc - 4])
                    b.wait("scalar", tok_vadd[(gc - 4) // 2])
                tok_sce[gc] = b.op("scalar", lambda E, gc=gc, sl=sl: E.activation(
                    expb.ap()[:, gc % 4, :], PA[sl], AF.Exp,
                    bias=zero_ap, scale=invsq_ap), "sce")
                b.wait("tensor", tok_sce[gc])
                if c == 0 and h > 0:
                    b.wait("tensor", tok_vec[h - 1])
                ih, jj = c // 16, c % 16
                b.op("tensor", lambda E, buf=buf, ih=ih, jj=jj, gc=gc, c=c: E.matmul(
                    P[4], Vraw.ap()[:, buf, ih, jj * 128:(jj + 1) * 128],
                    expb.ap()[:, gc % 4, 0:512], start=(c == 0), stop=(c == NKC - 1)))
                tok_tec[gc] = b.op("tensor", lambda E, buf=buf, ih=ih, jj=jj, gc=gc, c=c: E.matmul(
                    P[5], Vraw.ap()[:, buf, ih, jj * 128:(jj + 1) * 128],
                    expb.ap()[:, gc % 4, 512:1024], start=(c == 0), stop=(c == NKC - 1)), "tec")
                if c % 2 == 1:
                    k = c // 2
                    pr = gc // 2
                    b.wait("vector", tok_sce[gc])
                    if k == 0:
                        if h > 0:
                            b.wait("vector", tok_acast[h - 1])
                        tok_vadd[pr] = b.op("vector", lambda E, gc=gc: E.tensor_tensor(
                            sacc.ap(), expb.ap()[:, (gc - 1) % 4, :],
                            expb.ap()[:, gc % 4, :], op=ALU.add), "vadd")
                    else:
                        tok_vadd[pr] = b.op("vector", lambda E, gc=gc, pr=pr: E.tensor_tensor(
                            padd.ap()[:, pr % 2, :], expb.ap()[:, (gc - 1) % 4, :],
                            expb.ap()[:, gc % 4, :], op=ALU.add), "vadd")
                        b.wait("vector", tok_vadd[pr])
                        tok_vadd[pr] = b.op("vector", lambda E, pr=pr: E.tensor_tensor(
                            sacc.ap(), sacc.ap(), padd.ap()[:, pr % 2, :], op=ALU.add), "vadd")
                if c + 3 < NKC:
                    emit_scores(gc + 3, h, buf, c + 3)
                gc += 1
            b.wait("vector", tok_vadd[gc // 2 - 1])
            tok_acast[h] = b.op("vector", lambda E: E.tensor_copy(
                padd.ap()[:, 0, :], sacc.ap()), "acast")
            last_of_kh = (hh == 3)
            if last_of_kh and kh < NKH - 1:
                transpose_kh(kh + 1, extra_wait=tok_sce[gc - 1])
            b.wait("tensor", tok_acast[h])
            if h > 0:
                b.wait("tensor", tok_scr[h - 1])
            b.op("tensor", lambda E: E.matmul(
                P[6][0:1, :], onesc.ap(), padd.ap()[:, 0, 0:512], start=True, stop=True))
            tok_tesum[h] = b.op("tensor", lambda E: E.matmul(
                P[7][0:1, :], onesc.ap(), padd.ap()[:, 0, 512:1024], start=True, stop=True),
                "tesum")
            b.wait("vector", tok_tesum[h])
            b.wait("vector", tok_tec[gc - 1])

            def _recip(E):
                with nc.allow_low_precision(reason="bf16 softmax normalizer"):
                    return E.reciprocal(r_sb.ap(), PA[3][0:1, :])
            tok_ver = b.op("vector", _recip, "ver")
            b.wait("tensor", tok_ver)
            b.op("tensor", lambda E: E.matmul(P[6], onesrb.ap(), r_sb.ap()[:, 0:512],
                                              start=True, stop=True))
            tok_ter = b.op("tensor", lambda E: E.matmul(P[7], onesrb.ap(), r_sb.ap()[:, 512:1024],
                                                        start=True, stop=True), "ter")
            if h < NH - 1:
                if last_of_kh:
                    b.wait("tensor", tok_vek[kh + 1])
                    nbuf = (kh + 1) % 2
                else:
                    nbuf = buf
                emit_scores(gc, h + 1, nbuf, 0)
                emit_scores(gc + 1, h + 1, nbuf, 1)
            b.wait("scalar", tok_ter)
            tok_scr[h] = b.op("scalar", lambda E: E.copy(rb_sb.ap(), PA[3]), "scr")
            b.wait("vector", tok_scr[h])
            tok_vec[h] = b.op("vector", lambda E, h=h: E.tensor_tensor(
                qctx.ap()[:, h, :], PA[2], rb_sb.ap(), op=ALU.mult), "vec")
            if h < NH - 1:
                b.wait("tensor", tok_scr[h])
                emit_scores(gc + 2, h + 1, nbuf, 2)
            if last_of_kh and kh + 2 < NKH:
                gather_kh(kh + 2)
    b.barrier()

    def raw_stats_group(g):
        for tc in range(NT):
            b.op("vector", lambda E, tc=tc, g=g: E.bn_stats(
                STS[tc]["st"].ap()[:, g, :], raw.ap()[:, tc, g * 512:(g + 1) * 512]))

    def raw_stats_final(pfx):
        toks = {}
        b.drain("vector")
        for tc in range(NT):
            toks[tc] = b.op("vector", lambda E, S=STS[tc]: E.bn_aggr(
                S["mv"].ap(), S["st"].ap()[:, :, :]), pfx + "va")
        for tc in range(NT):
            S = STS[tc]
            b.wait("scalar", toks[tc])
            b.drain("scalar")
            b.op("scalar", lambda E, S=S: E.activation(S["rtmp"].ap(), S["mv"].ap()[:, 1:2],
                                                       AF.Ln, bias=eps_ap, scale=one_ap))
            b.drain("scalar")
            b.op("scalar", lambda E, S=S: E.activation(S["rstd"].ap(), S["rtmp"].ap(), AF.Exp,
                                                       bias=zero_ap, scale=neghalf_ap))
            b.drain("scalar")
            toks[tc] = b.op("scalar", lambda E, S=S: E.activation(
                S["mr"].ap(), S["mv"].ap()[:, 0:1], AF.Copy,
                bias=0.0, scale=S["rstd"].ap()), pfx + "sr")
        return toks

    tok_teo = {}
    tok_veo1 = {}
    tok_teo2 = {}
    tok_veo2 = {}

    def p4_transpose(k):
        b.wait("tensor", tok_veo1[k])
        if k >= 2:
            b.wait("tensor", tok_veo2[k - 2])
        for j in range(8):
            tok = b.op("tensor", lambda E, k=k, j=j: E.transpose(
                PBF[3][:, (k % 2) * 1024 + j * 128:(k % 2) * 1024 + (j + 1) * 128],
                aoT.ap()[:, k % 2, j * 128:(j + 1) * 128], idn.ap()),
                "teo2" if j == 7 else None)
        tok_teo2[k] = tok
        b.wait("vector", tok_teo2[k])
        tok_veo2[k] = b.op("vector", lambda E, k=k: E.tensor_copy(
            raw.ap()[:, 0:8, k * 128:(k + 1) * 128],
            PBF[3][:, (k % 2) * 1024:(k % 2) * 1024 + 1024].rearrange("p (a c) -> p a c", a=8)),
            "veo2")

    for oc in range(NH):
        if oc >= 2:
            b.wait("gpsimd", tok_teo[oc - 2])
        tok_slab = b.dma("gpsimd", lambda E, oc=oc: E.dma_start(
            wslabB.ap()[:, oc % 2, :, :], woP[oc].rearrange("p (dc o) -> p dc o", dc=ND)), "wo")
        b.wait("tensor", tok_slab)
        if oc >= 2:
            b.wait("tensor", tok_veo1[oc - 2])
        pi = oc % 2
        for dc in range(ND):
            b.op("tensor", lambda E, oc=oc, dc=dc, pi=pi: E.matmul(
                P[2 * pi], wslabB.ap()[:, oc % 2, dc, :], qctx.ap()[:, dc, 0:512],
                start=(dc == 0), stop=(dc == ND - 1)))
            tok = b.op("tensor", lambda E, oc=oc, dc=dc, pi=pi: E.matmul(
                P[2 * pi + 1], wslabB.ap()[:, oc % 2, dc, :], qctx.ap()[:, dc, 512:1024],
                start=(dc == 0), stop=(dc == ND - 1)),
                "teo" if dc == ND - 1 else None)
        tok_teo[oc] = tok
        b.wait("vector", tok_teo[oc])
        if oc >= 2:
            b.wait("vector", tok_teo2[oc - 2])
        tok_veo1[oc] = b.op("vector", lambda E, oc=oc, pi=pi: E.tensor_copy(
            aoT.ap()[:, oc % 2, :], PA[pi]), "veo1")
        if oc >= 1:
            p4_transpose(oc - 1)
        if oc % 4 == 0 and oc >= 4:
            b.wait("vector", tok_veo2[oc - 1])
            raw_stats_group((oc - 4) // 4)
    p4_transpose(NH - 1)
    b.wait("vector", tok_veo2[NH - 1])
    raw_stats_group(7)
    tok_p5pre = raw_stats_final("p5a")
    b.barrier()

    tok_x1w = {}
    tok_tr5 = {}
    tok_te5 = {}
    tok_p5sc = {}
    tok_scA = {}
    ln2v = [xst.ap()[:, i, :].bitcast(BF16)[:, 0:D] for i in range(2)]
    for it in range(NT + 1):
        if it < NT:
            tc = it
            if tc >= 2:
                b.wait("sync", tok_lnapply[("p5", tc - 2)])
                b.wait("sync", tok_tr5[(tc - 2, 1)])
            tok_x[("p5", tc)] = b.dma("sync", lambda E, tc=tc: E.dma_start(
                xst.ap()[:, tc % 2, :], xr[tc]), "x5l")
        if it >= 1:
            tc = it - 1
            SA = STS[tc]
            b.wait("vector", tok_scA[tc])
            b.wait("vector", tok_x[("p5", tc)])
            if tc >= 1:
                b.wait("vector", tok_x1w[tc - 1])
            b.drain("vector")
            tok_lnapply[("p5", tc)] = b.op("vector", lambda E, SA=SA, tc=tc: E.scalar_tensor_tensor(
                x1t.ap(), lnA.ap(), SA["mr"].ap(), xst.ap()[:, tc % 2, :],
                op0=ALU.subtract, op1=ALU.add), "p5x1")
            b.wait("sync", tok_lnapply[("p5", tc)])
            tok_x1w[tc] = b.dma("sync", lambda E, tc=tc: E.dma_start(x1s[tc], x1t.ap()), "x1w")
            tsc2 = ln_stats(lambda g: x1t.ap()[:, g * 512:(g + 1) * 512], "p5b", 8 + tc % 2)
            b.wait("vector", tsc2)
            if tc >= 1:
                b.wait("vector", tok_tr5[(tc - 1, 1)])
            SB = STS[8 + tc % 2]
            tok_apply2 = b.op("vector", lambda E, SB=SB, tc=tc: E.tensor_scalar(
                ln2v[tc % 2], x1t.ap(), SB["mv"].ap()[:, 0:1], SB["rstd"].ap(),
                op0=ALU.subtract, op1=ALU.mult), "p5ap2")
            transpose_to_lnT(lambda dc, tc=tc: ln2v[tc % 2][:, dc * 128:(dc + 1) * 128],
                             tc, tok_apply2, tok_tr5, tok_te5, "p5")
        if it < NT:
            tc = it
            b.wait("scalar", tok_p5pre[tc])
            if tc >= 1:
                b.wait("scalar", tok_lnapply[("p5", tc - 1)])
            tok_scA[tc] = b.op("scalar", lambda E, tc=tc, S=STS[tc]: E.activation(
                lnA.ap(), raw.ap()[:, tc, :], AF.Copy, bias=0.0, scale=S["rstd"].ap()), "p5sca")
    b.wait("sync", tok_x1w[NT - 2])
    b.wait("sync", tok_x1w[NT - 1])
    b.barrier()

    ta1 = b.dma("gpsimd", lambda E: E.dma_start(a1sb.ap(), a1d), "la1")
    ta3 = b.dma("gpsimd", lambda E: E.dma_start(a3sb.ap(), a3d), "la3")
    ta2 = b.dma("gpsimd", lambda E: E.dma_start(a2sb.ap(), a2d), "la2")
    for name, asb, xasb, tok in (("xa1", a1sb, xa1, ta1), ("xa3", a3sb, xa3, ta3)):
        b.wait("tensor", tok)
        for dc in range(ND):
            b.op("tensor", lambda E, asb=asb, dc=dc: E.matmul(
                P[6][0:16, :], asb.ap()[:, dc, :], lnT.ap()[:, dc, 0:512],
                start=(dc == 0), stop=(dc == ND - 1)))
            tok2 = b.op("tensor", lambda E, asb=asb, dc=dc: E.matmul(
                P[7][0:16, :], asb.ap()[:, dc, :], lnT.ap()[:, dc, 512:1024],
                start=(dc == 0), stop=(dc == ND - 1)),
                "texa" if dc == ND - 1 else None)
        b.wait("vector", tok2)
        tok_xa = b.op("vector", lambda E, xasb=xasb: E.tensor_scalar(
            xasb.ap(), PA[3][0:16, :], two_ap[0:16, :], None, op0=ALU.mult), "vexa")
        b.wait("tensor", tok_xa)

    tok_bst = {}
    tok_teh1 = {}
    tok_teh3 = {}
    tok_scf1 = {}
    tok_vef1 = {}
    tok_texg = {}
    tok_gtw = {}

    def emit_xg2(k):
        b.wait("tensor", tok_vef1[k])
        if k == 0:
            b.wait("tensor", ta2)
        b.op("tensor", lambda E, k=k: E.matmul(
            P[6][0:16, :], a2sb.ap()[:, k, :], gtt.ap()[:, k % 2, 0:512],
            start=(k == 0), stop=(k == NHC - 1)))
        tok_texg[k] = b.op("tensor", lambda E, k=k: E.matmul(
            P[7][0:16, :], a2sb.ap()[:, k, :], gtt.ap()[:, k % 2, 512:1024],
            start=(k == 0), stop=(k == NHC - 1)), "texg")

    for hc in range(NHC):
        grp = hc // 4
        if hc % 4 == 0:
            if grp >= 2:
                b.wait("gpsimd", tok_teh3[4 * (grp - 2) + 3])
            bw = min(512, HID - grp * 512)
            tok1 = b.dma("gpsimd", lambda E, grp=grp, bw=bw: E.dma_start(
                b1st.ap()[:, grp % 2, 0:bw], b1d[:, grp * 512:grp * 512 + bw]), "b1s")
            tok3 = b.dma("gpsimd", lambda E, grp=grp, bw=bw: E.dma_start(
                b3st.ap()[:, grp % 2, 0:bw], b3d[:, grp * 512:grp * 512 + bw]), "b3s")
            tok_bst[grp] = (tok1, tok3)
        if hc >= 2:
            b.wait("gpsimd", tok_teh3[hc - 2])
        tok_w1 = b.dma("gpsimd", lambda E, hc=hc: E.dma_start(
            w1slab.ap()[:, hc % 2, :, :], w1P[hc].rearrange("p (dc o) -> p dc o", dc=ND)), "w1l")
        tok_w3 = b.dma("gpsimd", lambda E, hc=hc: E.dma_start(
            w3slab.ap()[:, hc % 2, :, :], w3P[hc].rearrange("p (dc o) -> p dc o", dc=ND)), "w3l")
        b.wait("tensor", tok_w1)
        b.wait("tensor", tok_bst[grp][0])
        if hc >= 1:
            b.wait("tensor", tok_scf1[hc - 1])
        for dc in range(ND):
            b.op("tensor", lambda E, hc=hc, dc=dc: E.matmul(
                P[0], w1slab.ap()[:, hc % 2, dc, :], lnT.ap()[:, dc, 0:512],
                start=(dc == 0), stop=False))
            b.op("tensor", lambda E, hc=hc, dc=dc: E.matmul(
                P[1], w1slab.ap()[:, hc % 2, dc, :], lnT.ap()[:, dc, 512:1024],
                start=(dc == 0), stop=False))
        b.op("tensor", lambda E, hc=hc, grp=grp: E.matmul(
            P[0], b1st.ap()[:, grp % 2, (hc % 4) * 128:(hc % 4 + 1) * 128],
            xa1.ap()[:, 0:512], start=False, stop=True))
        tok_teh1[hc] = b.op("tensor", lambda E, hc=hc, grp=grp: E.matmul(
            P[1], b1st.ap()[:, grp % 2, (hc % 4) * 128:(hc % 4 + 1) * 128],
            xa1.ap()[:, 512:1024], start=False, stop=True), "teh1")
        if hc >= 1:
            emit_xg2(hc - 1)
        b.wait("tensor", tok_w3)
        b.wait("tensor", tok_bst[grp][1])
        if hc >= 1:
            b.wait("tensor", tok_vef1[hc - 1])
        for dc in range(ND):
            b.op("tensor", lambda E, hc=hc, dc=dc: E.matmul(
                P[2], w3slab.ap()[:, hc % 2, dc, :], lnT.ap()[:, dc, 0:512],
                start=(dc == 0), stop=False))
            b.op("tensor", lambda E, hc=hc, dc=dc: E.matmul(
                P[3], w3slab.ap()[:, hc % 2, dc, :], lnT.ap()[:, dc, 512:1024],
                start=(dc == 0), stop=False))
        b.op("tensor", lambda E, hc=hc, grp=grp: E.matmul(
            P[2], b3st.ap()[:, grp % 2, (hc % 4) * 128:(hc % 4 + 1) * 128],
            xa3.ap()[:, 0:512], start=False, stop=True))
        tok_teh3[hc] = b.op("tensor", lambda E, hc=hc, grp=grp: E.matmul(
            P[3], b3st.ap()[:, grp % 2, (hc % 4) * 128:(hc % 4 + 1) * 128],
            xa3.ap()[:, 512:1024], start=False, stop=True), "teh3")
        b.wait("scalar", tok_teh1[hc])
        if hc >= 1:
            b.wait("scalar", tok_vef1[hc - 1])
        tok_scf1[hc] = b.op("scalar", lambda E: E.activation(
            s1.ap(), PA[0], AF.Silu, bias=zero_ap, scale=one_ap), "scf1")
        b.wait("vector", tok_teh3[hc])
        b.wait("vector", tok_scf1[hc])
        if hc >= 2:
            b.wait("vector", tok_texg[hc - 2])
            b.wait("vector", tok_gtw[hc - 2])
        tok_vef1[hc] = b.op("vector", lambda E, hc=hc: E.tensor_tensor(
            gtt.ap()[:, hc % 2, :], PA[1], s1.ap(), op=ALU.mult), "vef1")
        b.wait("sync", tok_vef1[hc])
        tok_gtw[hc] = b.dma("sync", lambda E, hc=hc: E.dma_start(gts[hc], gtt.ap()[:, hc % 2, :]), "gtw")
    emit_xg2(NHC - 1)
    b.wait("vector", tok_texg[NHC - 1])
    tok_vexg = b.op("vector", lambda E: E.tensor_scalar(
        xg2.ap(), PA[3][0:16, :], two_ap[0:16, :], None, op0=ALU.mult), "vexg")
    b.wait("sync", tok_gtw[NHC - 2])
    b.wait("sync", tok_gtw[NHC - 1])
    b.barrier()
    tok_b2 = b.dma("gpsimd", lambda E: E.dma_start(b2sb.ap(), b2d), "b2l")

    tok_w2g = {}
    tok_gtr = {}
    tok_tew2c = {}
    tok_teog = {}
    tok_vew1 = {}
    tok_tew2t = {}
    tok_vew2 = {}

    WGEND = {0: 2, 1: 5, 2: 8, 3: 10}
    NW = len(W2GRP)
    NG = len(GTGRP)
    oc_slots = [(ogi, j) for ogi, (oc0, ow) in enumerate(OGS) for j in range(ow)]
    tew2c_val = lambda ogi, gg: ogi * NG + gg + 1

    for ogi, (oc0, ow) in enumerate(OGS):
        for wg, (h0, hw) in enumerate(W2GRP):
            gslab = ogi * NW + wg
            if gslab >= 2:
                pogi, pwg = divmod(gslab - 2, NW)
                b.wait("gpsimd", ("tew2c", tew2c_val(pogi, WGEND[pwg])))
            tok_w2g[(ogi, wg)] = b.dma("gpsimd", lambda E, h0=h0, hw=hw, oc0=oc0, ow=ow, wg=wg:
                                       E.dma_start(w2slab.ap()[:, wg % 2, 0:hw, 0:ow * 128],
                                                   w2P[:, h0:h0 + hw, oc0 * 128:oc0 * 128 + ow * 128]),
                                       "w2s")
        for gg, (g0, gw) in enumerate(GTGRP):
            ggl = ogi * NG + gg
            if ggl >= 2:
                pogi, pgg = divmod(ggl - 2, NG)
                b.wait("sync", ("tew2c", tew2c_val(pogi, pgg)))
            tok_gtr[(ogi, gg)] = b.dma("sync", lambda E, g0=g0, gw=gw, bufg=(ogi * NG + gg) % 2:
                                       E.dma_start(gt_rd.ap()[:, bufg, 0:gw, :],
                                                   gts[g0:g0 + gw].rearrange("g p t -> p g t")), "gtr")
        if ogi > 0:
            b.wait("tensor", tok_vew1[(ogi - 1, OGS[ogi - 1][1] - 1)])
        for hc in range(NHC):
            wg = next(i for i, (h0, hw) in enumerate(W2GRP) if h0 <= hc < h0 + hw)
            gg = hc // 8
            if hc == W2GRP[wg][0]:
                b.wait("tensor", tok_w2g[(ogi, wg)])
            if hc % 8 == 0:
                b.wait("tensor", tok_gtr[(ogi, gg)])
            for j in range(ow):
                lastg = (hc % 8 == 7 or hc == NHC - 1) and j == ow - 1
                bufg = (ogi * NG + gg) % 2
                b.op("tensor", lambda E, hc=hc, wg=wg, bufg=bufg, j=j: E.matmul(
                    P[2 * j], w2slab.ap()[:, wg % 2, hc - W2GRP[wg][0], j * 128:(j + 1) * 128],
                    gt_rd.ap()[:, bufg, hc % 8, 0:512], start=(hc == 0), stop=False))
                tok = b.op("tensor", lambda E, hc=hc, wg=wg, bufg=bufg, j=j: E.matmul(
                    P[2 * j + 1], w2slab.ap()[:, wg % 2, hc - W2GRP[wg][0], j * 128:(j + 1) * 128],
                    gt_rd.ap()[:, bufg, hc % 8, 512:1024], start=(hc == 0), stop=False),
                    ("tew2c" if lastg else None))
                if tok is not None:
                    tok_tew2c[(ogi, gg)] = tok
                    assert tok[1] == tew2c_val(ogi, gg), (tok, ogi, gg)
        b.wait("tensor", tok_vexg)
        b.wait("tensor", tok_b2)
        for j in range(ow):
            oc = oc0 + j
            b.op("tensor", lambda E, oc=oc, j=j: E.matmul(
                P[2 * j], b2sb.ap()[:, oc * 128:(oc + 1) * 128], xg2.ap()[:, 0:512],
                start=False, stop=True))
            tok = b.op("tensor", lambda E, oc=oc, j=j: E.matmul(
                P[2 * j + 1], b2sb.ap()[:, oc * 128:(oc + 1) * 128], xg2.ap()[:, 512:1024],
                start=False, stop=True), "teog" if j == ow - 1 else None)
        tok_teog[ogi] = tok
        for j in range(ow):
            oc = oc0 + j
            slot = oc_slots.index((ogi, j))
            b.wait("vector", tok_teog[ogi])
            prevsame = None
            for s in range(slot - 1, -1, -1):
                if oc_slots[s][1] % 2 == j % 2:
                    prevsame = oc_slots[s]
                    break
            if prevsame is not None:
                b.wait("vector", tok_tew2t[prevsame])
            tok_vew1[(ogi, j)] = b.op("vector", lambda E, j=j: E.tensor_copy(
                oTsb.ap()[:, j % 2, :], PA[j]), "vew1")
            b.wait("tensor", tok_vew1[(ogi, j)])
            if slot >= 2:
                b.wait("tensor", tok_vew2[oc_slots[slot - 2]])
            half = slot % 2
            for jj in range(8):
                tok = b.op("tensor", lambda E, j=j, jj=jj, half=half: E.transpose(
                    PBF[3][:, half * 1024 + jj * 128:half * 1024 + (jj + 1) * 128],
                    oTsb.ap()[:, j % 2, jj * 128:(jj + 1) * 128], idn.ap()),
                    "tew2t" if jj == 7 else None)
            tok_tew2t[(ogi, j)] = tok
            b.wait("vector", tok_tew2t[(ogi, j)])
            tok_vew2[(ogi, j)] = b.op("vector", lambda E, oc=oc, half=half: E.tensor_copy(
                raw.ap()[:, 0:8, oc * 128:(oc + 1) * 128],
                PBF[3][:, half * 1024:half * 1024 + 1024].rearrange("p (a c) -> p a c", a=8)),
                "vew2")
            if oc % 4 == 3:
                b.wait("vector", tok_vew2[(ogi, j)])
                raw_stats_group(oc // 4)
    tok_f3pre = raw_stats_final("f3a")
    b.barrier()

    tok_x1r = {}
    tok_yw = {}
    tok_out = {}
    tok_f3sc = {}
    tok_scF = {}
    for it in range(NT + 1):
        if it < NT:
            tc = it
            if tc >= 2:
                b.wait("sync", tok_out[tc - 2])
            tok_x1r[tc] = b.dma("sync", lambda E, tc=tc: E.dma_start(
                x1rd.ap()[:, tc % 2, :], x1s[tc]), "x1r")
        if it >= 1:
            tc = it - 1
            S = STS[tc]
            b.wait("vector", tok_scF[tc])
            b.wait("vector", tok_x1r[tc])
            if tc >= 1:
                b.wait("vector", tok_yw[tc - 1])
            b.drain("vector")
            tok_out[tc] = b.op("vector", lambda E, S=S, tc=tc: E.scalar_tensor_tensor(
                outT.ap(), lnF.ap(), S["mr"].ap(), x1rd.ap()[:, tc % 2, :],
                op0=ALU.subtract, op1=ALU.add), "veout")
            b.wait("sync", tok_out[tc])
            tok_yw[tc] = b.dma("sync", lambda E, tc=tc: E.dma_start(yr[tc], outT.ap()), "yw")
        if it < NT:
            tc = it
            b.wait("scalar", tok_f3pre[tc])
            if tc >= 1:
                b.wait("scalar", tok_out[tc - 1])
            tok_scF[tc] = b.op("scalar", lambda E, tc=tc, S=STS[tc]: E.activation(
                lnF.ap(), raw.ap()[:, tc, :], AF.Copy, bias=0.0, scale=S["rstd"].ap()), "f3sca")
    b.wait("sync", tok_yw[NT - 1])
    b.check()

    with nc.Block() as block:
        @block.sync
        def _(E):
            for f in b.ops["sync"]:
                f(E)

        @block.gpsimd
        def _(E):
            for f in b.ops["gpsimd"]:
                f(E)

        @block.tensor
        def _(E):
            for f in b.ops["tensor"]:
                f(E)

        @block.scalar
        def _(E):
            for f in b.ops["scalar"]:
                f(E)

        @block.vector
        def _(E):
            for f in b.ops["vector"]:
                f(E)

    stack.close()
    return nc


def kernel(L0_attn_hidden_states, L0_attn_k_cache, L0_attn_v_cache, L0_attn_Wq, L0_attn_Wo,
           L0_attn_block_table, L0_ffn_w1, L0_ffn_w3, L0_ffn_w2, L0_ffn_lora_A1, L0_ffn_lora_B1,
           L0_ffn_lora_A3, L0_ffn_lora_B3, L0_ffn_lora_A2, L0_ffn_lora_B2):
    x = np.ascontiguousarray(np.asarray(L0_attn_hidden_states, np.float32).reshape(8192, D))
    Wq = np.asarray(L0_attn_Wq, np.float32)
    Wo = np.asarray(L0_attn_Wo, np.float32)
    w1 = np.asarray(L0_ffn_w1, np.float32)
    w3 = np.asarray(L0_ffn_w3, np.float32)
    w2 = np.asarray(L0_ffn_w2, np.float32)

    wqP = np.ascontiguousarray(Wq.reshape(32, 128, ND, 128).transpose(0, 3, 2, 1)
                               .reshape(32, 128, ND * 128))
    woP = np.ascontiguousarray(Wo.reshape(32, 128, ND, 128).transpose(0, 3, 2, 1)
                               .reshape(32, 128, ND * 128))
    w1P = np.ascontiguousarray(w1.reshape(NHC, 128, ND, 128).transpose(0, 3, 2, 1)
                               .reshape(NHC, 128, ND * 128))
    w3P = np.ascontiguousarray(w3.reshape(NHC, 128, ND, 128).transpose(0, 3, 2, 1)
                               .reshape(NHC, 128, ND * 128))
    w2P = np.ascontiguousarray(w2.T.reshape(NHC, 128, D).transpose(1, 0, 2))
    a1P = np.ascontiguousarray(np.asarray(L0_ffn_lora_A1, np.float32).T
                               .reshape(ND, 128, 16).transpose(1, 0, 2))
    a3P = np.ascontiguousarray(np.asarray(L0_ffn_lora_A3, np.float32).T
                               .reshape(ND, 128, 16).transpose(1, 0, 2))
    a2P = np.ascontiguousarray(np.asarray(L0_ffn_lora_A2, np.float32).T
                               .reshape(NHC, 128, 16).transpose(1, 0, 2))
    b1T = np.ascontiguousarray(np.asarray(L0_ffn_lora_B1, np.float32).T)
    b3T = np.ascontiguousarray(np.asarray(L0_ffn_lora_B3, np.float32).T)
    b2T = np.ascontiguousarray(np.asarray(L0_ffn_lora_B2, np.float32).T)
    kc = np.ascontiguousarray(np.asarray(L0_attn_k_cache, np.float32).reshape(512, 16384))
    vc = np.ascontiguousarray(np.asarray(L0_attn_v_cache, np.float32).reshape(512, 16384))
    bt = np.ascontiguousarray(np.asarray(L0_attn_block_table, np.int32).reshape(2, 128))

    consts = np.zeros((128, 8), np.float32)
    consts[:, 0] = 1.0
    consts[:, 1] = EPS
    consts[:, 2] = INVSQ
    consts[:, 3] = -0.5
    consts[:, 4] = 2.0
    consts[:, 5] = 0.0
    idn = np.eye(128, dtype=np.float32)
    onesr = np.ones((1, 128), np.float32)

    invf = (1.0 / (10000.0 ** (np.arange(0, 128, 2, dtype=np.float32) / 128.0))).astype(np.float32)

    nc = build_program()
    in_maps = []
    for c in range(8):
        s0 = (c % 4) * 1024
        pos = np.arange(s0, s0 + 1024, dtype=np.float32)
        ang = np.concatenate([np.outer(invf, pos), np.outer(invf, pos)], 0)
        cosT = np.cos(ang).astype(np.float32)
        sinT = np.sin(ang).astype(np.float32)
        sinTs = np.concatenate([-sinT[0:64], sinT[64:128]], 0).astype(np.float32)
        m = {
            "x": np.ascontiguousarray(x[c * 1024:(c + 1) * 1024]),
            "wqP": wqP, "woP": woP, "w1P": w1P, "w3P": w3P, "w2P": w2P,
            "a1P": a1P, "a3P": a3P, "a2P": a2P, "b1T": b1T, "b3T": b3T, "b2T": b2T,
            "kc": kc, "vc": vc, "bt": bt,
            "cosT": np.ascontiguousarray(cosT), "sinTs": np.ascontiguousarray(sinTs),
            "idn": idn, "consts": consts, "onesr": onesr,
        }
        in_maps.append(m)
    res = run_bass_kernel_spmd(nc, in_maps, core_ids=list(range(8)))
    y = np.concatenate([np.asarray(r["y"], np.float32) for r in res.results], 0)
    return y.reshape(2, 4096, D)
